# version 17
# baseline (speedup 1.0000x reference)
import numpy as np
import ml_dtypes
import concourse.bass as bass
import concourse.mybir as mybir
from concourse.bass_utils import run_bass_kernel_spmd
from contextlib import ExitStack

F32 = mybir.dt.float32
BF16 = mybir.dt.bfloat16
AF = mybir.ActivationFunctionType
ALU = mybir.AluOpType
AX = mybir.AxisListType

S, D, HD, NH, NKV = 8192, 2048, 128, 16, 4
KVD = NKV * HD
TME = 2048
NCH = S // 128
NMY = TME // 128
NJ = D // 128
NQT = TME // 512
E, F, SF = 8, 1024, 1024
NFC = F // 128
SHIFT = 12.0
THETA = 10000.0
SBUF_BASE = 16576
SBUF_TOP = 229312
SBUF_BYTES = SBUF_TOP

ENGS = ("tensor", "scalar", "vector", "sync", "gpsimd")
OWN = {"tensor": "sPE", "scalar": "sSE", "vector": "sVE", "gpsimd": "sPO"}
SELFSYNC = ("scalar", "vector", "gpsimd")


class Plan:
    def __init__(self):
        self.ops = {e: [] for e in ENGS}
        self.cnt = {"sPE": 0, "sSE": 0, "sVE": 0, "sPO": 0}

    def op(self, eng, fn, waits=(), inc=None):
        waits = [w for w in waits if w is not None]
        if eng in SELFSYNC:
            inc = OWN[eng]
            waits.append((inc, self.cnt[inc]))
        if inc is not None:
            self.cnt[inc] += 1
        self.ops[eng].append((tuple(waits), fn, inc, 1))
        return (inc, self.cnt[inc]) if inc else None

    def dma(self, eng, fn, stream, waits=()):
        waits = [w for w in waits if w is not None]
        self.cnt[stream] = self.cnt.get(stream, 0) + 16
        self.ops[eng].append((tuple(waits), fn, stream, 16))
        return (stream, self.cnt[stream])

    def ms(self, sem):
        return (sem, self.cnt.get(sem, 0))

    def barrier(self):
        for eng in ("tensor", "scalar", "vector", "gpsimd"):
            s = OWN[eng]
            self.op(eng, lambda e: e.nop(), waits=[(s, self.cnt[s])], inc=s)
        counts = list(self.cnt.items())
        for eng in ENGS:
            self.ops[eng].append((tuple(counts), None, None, 0))


def build_nc():
    nc = bass.Bass("TRN2", target_bir_lowering=False)
    din = {}

    def dI(name, shape, dt=F32):
        din[name] = nc.dram_tensor(name, list(shape), dt, kind="ExternalInput").ap()
        return din[name]

    xfull = dI("xfull", (S, D))
    xmine = dI("xmine", (TME, D))
    wq = dI("wq", (D, D)); wk = dI("wk", (D, KVD)); wv = dI("wv", (D, KVD))
    wo = dI("wo", (D, D))
    rw = dI("rw", (D, 1))
    rtw = dI("rtw", (D, E))
    wg = dI("wg", (E, D, F)); wu = dI("wu", (E, D, F)); wd = dI("wd", (E, F, D))
    swg = dI("swg", (D, SF)); swu = dI("swu", (D, SF)); swd = dI("swd", (SF, D))
    rkc = dI("rkc", (S, KVD)); rks = dI("rks", (S, KVD))
    rqc = dI("rqc", (TME, D)); rqs = dI("rqs", (TME, D))
    idb = dI("idb", (128, 128), BF16)
    idf = dI("idf", (128, 128), F32)
    dout = nc.dram_tensor("out", [TME, D], F32, kind="ExternalOutput").ap()

    kr_sp = nc.dram_tensor("kr_sp", [S, KVD], BF16, kind="Internal").ap()
    v_sp = nc.dram_tensor("v_sp", [NCH, 128, KVD], BF16, kind="Internal").ap()
    qr_sp = nc.dram_tensor("qr_sp", [TME, D], BF16, kind="Internal").ap()
    at_sp = nc.dram_tensor("at_sp", [NH, 128, TME], BF16, kind="Internal").ap()
    wf_sp = nc.dram_tensor("wf_sp", [18, NJ, 128, F], BF16, kind="Internal").ap()

    H_OFF = SBUF_TOP - 65536
    MOE_OFF = H_OFF - 65536
    galloc_off = [SBUF_BASE]

    def galloc(name, shape, dt, align=64):
        nb = {F32: 4, BF16: 2}[dt] * int(np.prod(shape[1:]))
        o = (galloc_off[0] + align - 1) // align * align
        galloc_off[0] = o + nb
        return nc.alloc_sbuf_tensor_at(name, list(shape), dt, offset=o)

    sid_b = galloc("sid_b", (128, 128), BF16)
    sid_f = galloc("sid_f", (128, 128), F32)
    ones_c = galloc("ones_c", (128, 1), BF16)
    ones_cf = galloc("ones_cf", (128, 1), F32)
    mean_c = galloc("mean_c", (128, 1), BF16)
    onesr = galloc("onesr", (1, 128), F32)
    onesr_b = galloc("onesr_b", (1, 128), BF16)
    cexpb = galloc("cexpb", (128, 1), F32)
    chn = galloc("chn", (1, 1), F32)
    chp = galloc("chp", (1, 1), F32)
    rw_sb = galloc("rw_sb", (128, NJ), F32)
    rsinv_row = galloc("rsinv_row", (1, TME), F32)
    GEND = (galloc_off[0] + 63) // 64 * 64

    def mkalloc(tag, limit=SBUF_TOP):
        st = [GEND]

        def alloc(name, shape, dt, align=64):
            nb = {F32: 4, BF16: 2}[dt] * int(np.prod(shape[1:]))
            o = (st[0] + align - 1) // align * align
            st[0] = o + nb
            assert st[0] <= limit, (tag, name, st[0], limit)
            return nc.alloc_sbuf_tensor_at(f"{tag}_{name}", list(shape), dt, offset=o)
        return alloc

    h_sb = nc.alloc_sbuf_tensor_at("h_sb", [128, NJ * 2048], BF16, offset=H_OFF)
    moe = nc.alloc_sbuf_tensor_at("moe", [128, NJ * 2048], BF16, offset=MOE_OFF)

    pA = nc.alloc_psum_tensor("pA", [128, 2048], F32)
    pB = nc.alloc_psum_tensor("pB", [128, 512], F32)
    pC = nc.alloc_psum_tensor("pC", [128, 512], F32)
    pD = nc.alloc_psum_tensor("pD", [128, 512], F32)
    pE = nc.alloc_psum_tensor("pE", [128, 512], F32)

    P = Plan()
    W = lambda *ms: [m for m in ms if m is not None]

    P.dma("sync", lambda e: e.dma_start(out=sid_b[:], in_=idb), "dId")
    id_ready = P.dma("sync", lambda e: e.dma_start(out=sid_f[:], in_=idf), "dId")
    rw_ld = P.dma("sync", lambda e: e.dma_start(
        out=rw_sb[:], in_=rw.rearrange("(j p) o -> p (j o)", p=128),
        allow_slow_non_contiguous=True), "dRw")
    P.op("vector", lambda e: e.memset(ones_c[:], 1.0))
    P.op("vector", lambda e: e.memset(ones_cf[:], 1.0))
    P.op("vector", lambda e: e.memset(mean_c[:], 1.0 / 2048.0))
    P.op("vector", lambda e: e.memset(onesr[:], 1.0))
    P.op("vector", lambda e: e.memset(onesr_b[:], 1.0))
    P.op("vector", lambda e: e.memset(cexpb[:], -SHIFT))
    P.op("vector", lambda e: e.memset(chn[:], -0.5))
    P.op("vector", lambda e: e.memset(chp[:], 0.5), inc="sVE")
    init_ve = P.ms("sVE")

    al = mkalloc("p1")
    wq_sb = al("wq", (128, NJ * 2048), BF16)
    wk_sb = al("wk", (128, NJ * 512), BF16)
    wv_sb = al("wv", (128, NJ * 512), BF16)
    xbf = [al(f"xbf{i}", (128, 2048), BF16) for i in range(2)]
    xt = [al(f"xt{i}", (128, 2048), BF16) for i in range(2)]
    kf = al("kf", (128, 512), F32)
    km1 = al("km1", (128, 512), F32)
    km2 = al("km2", (128, 512), F32)
    krot = [al(f"krot{i}", (128, 512), BF16) for i in range(2)]
    vt = [al(f"vt{i}", (128, 512), BF16) for i in range(2)]
    tbc = al("tbc", (128, 512), F32)
    tbs = al("tbs", (128, 512), F32)
    xf32 = [al(f"xf32{i}", (128, 2048), F32) for i in range(2)]
    qf = al("qf", (128, 2048), F32)
    qm1 = al("qm1", (128, 2048), F32)
    qm2 = al("qm2", (128, 512), F32)
    qrot = al("qrot", (128, 2048), BF16)
    wtmp = [al(f"wtmp{i}", (128, NJ * 128), BF16) for i in range(3)]
    rmf = al("rmf", (128, NJ * 128), BF16)

    P.op("vector", lambda e: e.memset(rmf[:], 1.0), inc="sVE")
    for j in range(NJ):
        P.op("vector", lambda e, j=j: e.tensor_scalar(
            rmf[:, j * 128:(j + 1) * 128], rmf[:, j * 128:(j + 1) * 128],
            rw_sb[:, j:j + 1], None, ALU.mult),
            waits=[rw_ld] if j == 0 else (), inc="sVE")
    rmf_ready = P.ms("sVE")

    P.dma("gpsimd", lambda e: e.dma_start(
        out=wk_sb[:].rearrange("p (j n) -> p j n", n=512),
        in_=wk.rearrange("(j p) n -> p j n", p=128)), "dWqkv")
    P.dma("gpsimd", lambda e: e.dma_start(
        out=wv_sb[:].rearrange("p (j n) -> p j n", n=512),
        in_=wv.rearrange("(j p) n -> p j n", p=128)), "dWqkv")
    w_ready = P.dma("gpsimd", lambda e: e.dma_start(
        out=wq_sb[:].rearrange("p (j n) -> p j n", n=2048),
        in_=wq.rearrange("(j p) n -> p j n", p=128)), "dWqkv")

    fold_jobs = []
    for m in range(18):
        src = wg[m] if m < 8 else (wu[m - 8] if m < 16 else (swg if m == 16 else swu))
        for sl in range(8):
            fold_jobs.append((m, sl, src))
    fold_i = [0]
    fold_busy = [None, None, None]

    def fold_step():
        i = fold_i[0]
        if i >= len(fold_jobs):
            return
        fold_i[0] = i + 1
        m, sl, src = fold_jobs[i]
        buf = i % 3
        wt = wtmp[buf]
        ld = P.dma("gpsimd", lambda e, wt=wt, src=src, sl=sl: e.dma_start(
            out=wt[:].rearrange("p (j f) -> p j f", f=128),
            in_=src[:, sl * 128:(sl + 1) * 128].rearrange("(j p) f -> p j f", p=128)),
            f"dWtmp{buf}", waits=W(fold_busy[buf]))
        folded = P.op("vector", lambda e, wt=wt: e.tensor_tensor(
            wt[:], wt[:], rmf[:], ALU.mult), waits=[ld, rmf_ready])
        fold_busy[buf] = P.dma("sync", lambda e, wt=wt, m=m, sl=sl: e.dma_start(
            out=wf_sp[m, :, :, sl * 128:(sl + 1) * 128].transpose([1, 0, 2]),
            in_=wt[:].rearrange("p (j f) -> p j f", f=128)),
            f"dFold{buf}", waits=[folded])

    xbf_free = [None, None]
    xt_free = [None, None]
    pD_free = None
    pE_free = None
    pK_free = None
    pV_free = None
    pA_free = None
    kf_free = None
    qf_free = None
    qrot_free = [None]
    krot_free = [None, None]
    vt_free = [None, None]
    tb_free = None
    kt_spilled = None
    qt_spilled = None
    strip_evs = []
    qstrip_evs = []

    pA_free_p1 = [None]

    xt_rdy = {}
    cast_rdy = {}
    xf32_free = [None, None]

    def load_cast(c):
        mine = c >= NCH
        cc = c - NCH if mine else c
        buf = c % 2
        src = (xmine if mine else xfull)[cc * 128:(cc + 1) * 128, :]
        x_ld = P.dma("sync", lambda e, b=buf, src=src: e.dma_start(
            out=xf32[b][:], in_=src), f"dXf{buf}", waits=W(xf32_free[buf]))
        P.op("vector", lambda e, b=buf: e.tensor_copy(xbf[b][:], xf32[b][:]),
             waits=[x_ld] + W(xbf_free[buf]))
        cast_rdy[buf] = P.ms("sVE")
        xf32_free[buf] = cast_rdy[buf]

    def prep_t(c):
        nonlocal pD_free, pE_free
        buf = c % 2
        for j in range(NJ):
            tgt, colo = (pD, j * 64) if j < 8 else (pE, (j - 8) * 64)
            wt0 = W(cast_rdy[buf], pD_free, pE_free, init_ve, id_ready) if j == 0 else ()
            P.op("tensor", lambda e, b=buf, j=j, tgt=tgt, colo=colo: e.transpose(
                tgt[:, colo:colo + 64].bitcast(BF16),
                xbf[b][:, j * 128:(j + 1) * 128], sid_b[:]),
                waits=wt0, inc="sPE" if j in (7, 15) else None)
            if j == 7:
                tpD = P.ms("sPE")
        tpE = P.ms("sPE")
        xbf_free[buf] = tpE
        P.op("scalar", lambda e, b=buf: e.copy(xt[b][:, 0:1024], pD[:].bitcast(BF16)),
             waits=[tpD] + W(xt_free[buf]), inc="sSE")
        xt_r1 = P.ms("sSE")
        P.op("scalar", lambda e, b=buf: e.copy(xt[b][:, 1024:2048], pE[:].bitcast(BF16)),
             waits=[tpE], inc="sSE")
        xt_ready = P.ms("sSE")
        pD_free = xt_ready
        pE_free = xt_ready
        xt_rdy[buf] = (xt_r1, xt_ready)

    load_cast(0)
    load_cast(1)
    prep_t(0)
    for c in range(NCH + NMY):
        mine = c >= NCH
        cc = c - NCH if mine else c
        buf = c % 2
        xt_r1, xt_ready = xt_rdy[buf]

        if not mine:
            for j in range(NJ):
                wj = ()
                if j == 0:
                    wj = [xt_r1, w_ready] + W(pK_free)
                elif j == 8:
                    wj = [xt_ready]
                P.op("tensor", lambda e, b=buf, j=j: e.matmul(
                    pB[:], xt[b][:, j * 128:(j + 1) * 128],
                    wk_sb[:, j * 512:(j + 1) * 512],
                    start=(j == 0), stop=(j == NJ - 1)),
                    waits=wj, inc="sPE" if j == NJ - 1 else None)
            k_done = P.ms("sPE")
            if c + 2 < NCH + NMY:
                load_cast(c + 2)
            if c + 1 < NCH + NMY:
                prep_t(c + 1)
            for j in range(NJ):
                P.op("tensor", lambda e, b=buf, j=j: e.matmul(
                    pC[:], xt[b][:, j * 128:(j + 1) * 128],
                    wv_sb[:, j * 512:(j + 1) * 512],
                    start=(j == 0), stop=(j == NJ - 1)),
                    waits=W(pV_free) if j == 0 else (),
                    inc="sPE" if j == NJ - 1 else None)
            kv_done = P.ms("sPE")
            xt_free[buf] = kv_done
            P.op("scalar", lambda e: e.copy(kf[:], pB[:]),
                 waits=[k_done] + W(kf_free), inc="sSE")
            kf_ready = P.ms("sSE")
            pK_free = kf_ready
            P.op("scalar", lambda e, b=buf: e.copy(vt[b][:], pC[:]),
                 waits=[kv_done] + W(vt_free[buf]), inc="sSE")
            v_ready = P.ms("sSE")
            pV_free = v_ready
            vt_free[buf] = P.dma("scalar", lambda e, b=buf, c=c: e.dma_start(
                out=v_sp[c], in_=vt[b][:]), f"dVsp{buf}", waits=[v_ready])
            P.dma("sync", lambda e, cc=cc: e.dma_start(
                out=tbc[:], in_=rkc[cc * 128:(cc + 1) * 128, :]),
                "dTb", waits=W(tb_free))
            tb_ready = P.dma("sync", lambda e, cc=cc: e.dma_start(
                out=tbs[:], in_=rks[cc * 128:(cc + 1) * 128, :]), "dTb")
            P.op("vector", lambda e: e.tensor_tensor(km1[:], kf[:], tbc[:], ALU.mult),
                 waits=[kf_ready, tb_ready], inc="sVE")
            P.op("vector", lambda e: e.tensor_tensor(
                km2[:].rearrange("p (h s x) -> p h s x", h=4, s=2)[:, :, 0, :],
                kf[:].rearrange("p (h s x) -> p h s x", h=4, s=2)[:, :, 1, :],
                tbs[:].rearrange("p (h s x) -> p h s x", h=4, s=2)[:, :, 0, :],
                ALU.mult), inc="sVE")
            P.op("vector", lambda e: e.tensor_tensor(
                km2[:].rearrange("p (h s x) -> p h s x", h=4, s=2)[:, :, 1, :],
                kf[:].rearrange("p (h s x) -> p h s x", h=4, s=2)[:, :, 0, :],
                tbs[:].rearrange("p (h s x) -> p h s x", h=4, s=2)[:, :, 1, :],
                ALU.mult), inc="sVE")
            P.op("vector", lambda e, b=buf: e.tensor_tensor(
                krot[b][:], km1[:], km2[:], ALU.add),
                waits=W(krot_free[buf]), inc="sVE")
            rope_done = P.ms("sVE")
            kf_free = rope_done
            tb_free = rope_done
            krot_free[buf] = P.dma("gpsimd", lambda e, b=buf, c=c: e.dma_start(
                out=kr_sp[c * 128:(c + 1) * 128, :], in_=krot[b][:]),
                f"dKr{buf}", waits=[rope_done])
        else:
            for qb in range(4):
                for j in range(NJ):
                    P.op("tensor", lambda e, b=buf, j=j, qb=qb: e.matmul(
                        pA[:, qb * 512:(qb + 1) * 512],
                        xt[b][:, j * 128:(j + 1) * 128],
                        wq_sb[:, j * 2048 + qb * 512: j * 2048 + (qb + 1) * 512],
                        start=(j == 0), stop=(j == NJ - 1)),
                        waits=([xt_r1, w_ready] + W(pA_free_p1[0]))
                              if (qb == 0 and j == 0) else
                              ([xt_ready] if (qb == 0 and j == 8) else ()),
                        inc="sPE" if (qb == 3 and j == NJ - 1) else None)
            q_done = P.ms("sPE")
            xt_free[buf] = q_done
            P.op("scalar", lambda e: e.copy(qf[:, 0:1024], pA[:, 0:1024]),
                 waits=[q_done] + W(qf_free), inc="sSE")
            P.op("scalar", lambda e: e.copy(qf[:, 1024:2048], pA[:, 1024:2048]), inc="sSE")
            qf_ready = P.ms("sSE")
            pA_free_p1[0] = qf_ready
            if c + 2 < NCH + NMY:
                load_cast(c + 2)
            if c + 1 < NCH + NMY:
                prep_t(c + 1)
            last_ms = None
            for si in range(4):
                P.dma("sync", lambda e, cc=cc, si=si: e.dma_start(
                    out=tbc[:], in_=rqc[cc * 128:(cc + 1) * 128, si * 512:(si + 1) * 512]),
                    "dTb", waits=W(tb_free))
                tb_ready = P.dma("sync", lambda e, cc=cc, si=si: e.dma_start(
                    out=tbs[:], in_=rqs[cc * 128:(cc + 1) * 128, si * 512:(si + 1) * 512]),
                    "dTb")
                sl = slice(si * 512, (si + 1) * 512)
                P.op("vector", lambda e, sl=sl: e.tensor_tensor(
                    qm1[:, sl], qf[:, sl], tbc[:], ALU.mult),
                    waits=[qf_ready, tb_ready], inc="sVE")
                P.op("vector", lambda e, sl=sl: e.tensor_tensor(
                    qm2[:].rearrange("p (h s x) -> p h s x", h=4, s=2)[:, :, 0, :],
                    qf[:, sl].rearrange("p (h s x) -> p h s x", h=4, s=2)[:, :, 1, :],
                    tbs[:].rearrange("p (h s x) -> p h s x", h=4, s=2)[:, :, 0, :],
                    ALU.mult), inc="sVE")
                P.op("vector", lambda e, sl=sl: e.tensor_tensor(
                    qm2[:].rearrange("p (h s x) -> p h s x", h=4, s=2)[:, :, 1, :],
                    qf[:, sl].rearrange("p (h s x) -> p h s x", h=4, s=2)[:, :, 0, :],
                    tbs[:].rearrange("p (h s x) -> p h s x", h=4, s=2)[:, :, 1, :],
                    ALU.mult), inc="sVE")
                P.op("vector", lambda e, sl=sl: e.tensor_tensor(
                    qrot[:, sl], qm1[:, sl], qm2[:], ALU.add),
                    waits=W(qrot_free[0]) if si == 0 else (), inc="sVE")
                last_ms = P.ms("sVE")
                tb_free = last_ms
            qrope_done = last_ms
            qf_free = qrope_done
            qrot_free[0] = P.dma("gpsimd", lambda e, cc=cc: e.dma_start(
                out=qr_sp[cc * 128:(cc + 1) * 128, :], in_=qrot[:]),
                "dQr", waits=[qrope_done])
        fold_step()
        fold_step()
    while fold_i[0] < len(fold_jobs):
        fold_step()

    P.barrier()

    al = mkalloc("p2")
    ktb = [al(f"ktb{i}", (128, S), BF16) for i in range(2)]
    vtb = [al(f"vtb{i}", (128, NCH * 128), BF16) for i in range(2)]
    qtb = [al(f"qtb{i}", (128, TME), BF16) for i in range(4)]
    expS = [al(f"expS{i}", (128, 2048), BF16) for i in range(2)]
    sumacc = al("sumacc", (128, 2048), F32)
    sumacc2 = al("sumacc2", (128, 2048), F32)
    av_f = [al(f"avf{i}", (128, 512), F32) for i in range(2)]
    rep_f = [al(f"repf{i}", (128, 512), F32) for i in range(2)]
    att_t = [al(f"att{i}", (128, 512), BF16) for i in range(4)]
    d_row = [al(f"drow{i}", (1, 512), F32) for i in range(2)]

    kv_free = [None, None]
    kv_ready = [None, None]
    qt_free = [None] * 4
    qt_ready = [None] * 4
    expS_free = [None, None]
    att_free = [None] * 4
    avf_free = [None, None]
    repf_free = [None, None]
    drow_free = [None, None]

    def prefetch_kv(g):
        b = g % 2
        P.dma("sync", lambda e, g=g, b=b: e.dma_start(
            out=ktb[b][:], in_=kr_sp[:, g * 128:(g + 1) * 128], transpose=True),
            f"dKv{b}", waits=W(kv_free[b]))
        kv_ready[b] = P.dma("sync", lambda e, g=g, b=b: e.dma_start(
            out=vtb[b][:].rearrange("p (c x) -> p c x", x=128),
            in_=v_sp[:, :, g * 128:(g + 1) * 128].transpose([1, 0, 2])), f"dKv{b}")

    def prefetch_q(h, slot):
        qt_ready[slot] = P.dma("sync", lambda e, h=h, slot=slot: e.dma_start(
            out=qtb[slot][:], in_=qr_sp[:, h * 128:(h + 1) * 128], transpose=True),
            f"dQs{slot}", waits=W(qt_free[slot]))

    prefetch_kv(0)
    prefetch_q(0, 0)
    prefetch_q(1, 1)
    pA_free2 = None
    exph_ms = [None, None]
    pDE_free = [None]
    sumacc_free = [None]
    pBC_free2 = [None, None]
    pend_epi = []

    def mk_epi(sum_done_v, sum_done_p, avf0, avf1, h0, h1, qt):
        box = {'avf': [avf0, avf1]}

        def s1():
            for hi in range(2):
                for k2 in range(2):
                    P.op("tensor", lambda e, hi=hi, k2=k2: e.matmul(
                        (pD if hi == 0 else pE)[0:1, :], ones_cf[:],
                        sumacc[:, hi * 1024 + k2 * 512: hi * 1024 + (k2 + 1) * 512],
                        start=(k2 == 0), stop=(k2 == 1)),
                        waits=([sum_done_v] + W(pDE_free[0]))
                              if (hi == 0 and k2 == 0) else (),
                        inc="sPE" if (hi == 1 and k2 == 1) else None)
            box['sums'] = P.ms("sPE")
            sumacc_free[0] = box['sums']

        def s2():
            P.op("vector", lambda e: e.tensor_copy(d_row[0][:], pD[0:1, :]),
                 waits=[box['sums']] + W(drow_free[0], drow_free[1]))
            P.op("vector", lambda e: e.tensor_copy(d_row[1][:], pE[0:1, :]))
            P.op("vector", lambda e: e.reciprocal(d_row[0][:], d_row[0][:]))
            P.op("vector", lambda e: e.reciprocal(d_row[1][:], d_row[1][:]))
            box['rcp'] = P.ms("sVE")

        def s3():
            P.op("tensor", lambda e: e.matmul(
                pD[:], onesr[:], d_row[0][:], start=True, stop=True),
                waits=W(box['rcp']))
            P.op("tensor", lambda e: e.matmul(
                pE[:], onesr[:], d_row[1][:], start=True, stop=True),
                inc="sPE")
            box['reps'] = P.ms("sPE")
            drow_free[0] = box['reps']
            drow_free[1] = box['reps']

        def s4():
            for hi in range(2):
                P.op("scalar", lambda e, hi=hi: e.copy(
                    rep_f[hi][:], (pD if hi == 0 else pE)[:]),
                    waits=([box['reps']] if hi == 0 else []) + W(repf_free[hi]),
                    inc="sSE")
            box['evs'] = P.ms("sSE")
            pDE_free[0] = box['evs']

        def s5():
            for hi, hh in enumerate((h0, h1)):
                slot = (qt * 2 + hi) % 4
                P.op("vector", lambda e, hi=hi, slot=slot: e.tensor_tensor(
                    att_t[slot][:], av_f[hi][:], rep_f[hi][:], ALU.mult),
                    waits=[box['evs'], box['avf'][hi]] + W(att_free[slot]))
                nrm = P.ms("sVE")
                avf_free[hi] = nrm
                repf_free[hi] = nrm
                att_free[slot] = P.dma("sync",
                    lambda e, hh=hh, qt=qt, slot=slot: e.dma_start(
                        out=at_sp[hh][:, qt * 512:(qt + 1) * 512],
                        in_=att_t[slot][:]),
                    f"dAt{slot}", waits=[nrm])

        return [(1, s1), (3, s2), (5, s3), (7, s4), (9, s5)]

    for g in range(NKV):
        gb = g % 2
        if g + 1 < NKV:
            prefetch_kv(g + 1)
        for pr in range(2):
            pi = g * 2 + pr
            h0, h1 = 4 * g + 2 * pr, 4 * g + 2 * pr + 1
            s0, s1 = (2 * pi) % 4, (2 * pi + 1) % 4
            if pi + 1 < 8:
                prefetch_q(h0 + 2 if pr == 0 else 4 * (g + 1), (s0 + 2) % 4)
                prefetch_q((h0 + 3) if pr == 0 else 4 * (g + 1) + 1, (s1 + 2) % 4)
            for qt in range(NQT):
                av_ms = [None, None]
                for kk in range(32):
                    kc = kk * 2
                    eb = expS[(kk - 1) % 2]
                    kcp = (kk - 1) * 2
                    first = (kk - 1 == 0)
                    while pend_epi and kk >= pend_epi[0][0]:
                        pend_epi.pop(0)[1]()
                    if kk >= 1:
                        seng = "vector"
                        sacc = sumacc
                        if kk - 1 == 0:
                            P.op(seng, lambda e, eb=eb, sacc=sacc: e.tensor_copy(
                                sacc[:], eb[:]),
                                waits=W(exph_ms[0], exph_ms[1], sumacc_free[0]))
                        else:
                            P.op(seng, lambda e, eb=eb, sacc=sacc: e.tensor_tensor(
                                sacc[:], sacc[:], eb[:], ALU.add),
                                waits=W(exph_ms[0], exph_ms[1]))
                        sum_ms_v = P.ms("sVE")
                    for hi, ss in ((0, s0), (1, s1)):
                        w0 = W(exph_ms[hi])
                        if kk == 0:
                            w0 += W(kv_ready[gb], qt_ready[ss], pA_free2)
                        for k2 in range(2):
                            P.op("tensor", lambda e, gb=gb, kc=kc, k2=k2, ss=ss, hi=hi, qt=qt: e.matmul(
                                pA[:, hi * 1024 + k2 * 512: hi * 1024 + (k2 + 1) * 512],
                                ktb[gb][:, (kc + k2) * 128:(kc + k2 + 1) * 128],
                                qtb[ss][:, qt * 512:(qt + 1) * 512],
                                start=True, stop=True),
                                waits=w0 if k2 == 0 else (),
                                inc="sPE" if k2 == 1 else None)
                        sh_ms = P.ms("sPE")
                        if kk >= 1:
                            for k2 in range(2):
                                P.op("tensor", lambda e, gb=gb, kcp=kcp, k2=k2, hi=hi, eb=eb, first=first: e.matmul(
                                    (pB if hi == 0 else pC)[:],
                                    vtb[gb][:, (kcp + k2) * 128:(kcp + k2 + 1) * 128],
                                    eb[:, hi * 1024 + k2 * 512: hi * 1024 + (k2 + 1) * 512],
                                    start=(first and k2 == 0), stop=False),
                                    waits=W(pBC_free2[hi]) if (first and k2 == 0) else (),
                                    inc="sPE" if k2 == 1 else None)
                            av_ms[hi] = P.ms("sPE")
                        we = [sh_ms]
                        if hi == 0 and expS_free[kk % 2]:
                            we += expS_free[kk % 2]
                        P.op("scalar", lambda e, kk=kk, hi=hi: e.activation(
                            expS[kk % 2][:, hi * 1024:(hi + 1) * 1024],
                            pA[:, hi * 1024:(hi + 1) * 1024], AF.Exp, bias=cexpb[:]),
                            waits=we, inc="sSE")
                        exph_ms[hi] = P.ms("sSE")
                    if kk >= 1:
                        expS_free[(kk - 1) % 2] = [av_ms[0], av_ms[1], sum_ms_v]
                eb = expS[31 % 2]
                kcp = 62
                for hi in range(2):
                    for k2 in range(2):
                        P.op("tensor", lambda e, gb=gb, kcp=kcp, k2=k2, hi=hi, eb=eb: e.matmul(
                            (pB if hi == 0 else pC)[:],
                            vtb[gb][:, (kcp + k2) * 128:(kcp + k2 + 1) * 128],
                            eb[:, hi * 1024 + k2 * 512: hi * 1024 + (k2 + 1) * 512],
                            start=False, stop=(k2 == 1)),
                            waits=[exph_ms[hi]] if k2 == 0 else (),
                            inc="sPE" if k2 == 1 else None)
                    av_ms[hi] = P.ms("sPE")
                P.op("vector", lambda e, eb=eb: e.tensor_tensor(
                    sumacc[:], sumacc[:], eb[:], ALU.add),
                    waits=W(exph_ms[0], exph_ms[1]))
                sum_done_v = P.ms("sVE")
                sum_done_p = P.ms("sPO")
                expS_free[1] = [av_ms[0], av_ms[1], sum_done_v]
                avf_in = [None, None]
                for hi in range(2):
                    P.op("scalar", lambda e, hi=hi: e.copy(
                        av_f[hi][:], pB[:] if hi == 0 else pC[:]),
                        waits=[av_ms[hi]] + W(avf_free[hi]), inc="sSE")
                    avf_in[hi] = P.ms("sSE")
                    pBC_free2[hi] = avf_in[hi]
                pend_epi.extend(mk_epi(sum_done_v, sum_done_p,
                                       avf_in[0], avf_in[1], h0, h1, qt))
            pairdone = P.ms("sPE")
            qt_free[s0] = pairdone
            qt_free[s1] = pairdone
        kv_free[gb] = P.ms("sPE")
    while pend_epi:
        pend_epi.pop(0)[1]()

    P.barrier()

    al = mkalloc("p3", limit=H_OFF)
    wo_sb = al("wo", (128, NJ * 2048), BF16)
    abuf = al("abuf", (128, NH * 512), BF16)
    x1pre = al("x1pre", (128, NJ * 512), BF16)
    xres = [al(f"xres{i}", (128, 2048), F32) for i in range(2)]
    wo_f = [al(f"wof{i}", (128, 512), F32) for i in range(2)]
    sq_t = al("sq", (128, 512), BF16)
    mrep_f = al("mrep", (128, 512), F32)
    rxrep_f = al("rxrep", (128, 512), F32)
    rows = al("rows", (1, 6 * 512), F32)
    r_m = rows[0:1, 0:512]; r_s = rows[0:1, 512:1024]; r_t = rows[0:1, 1024:1536]
    r_u = rows[0:1, 1536:2048]; r_q = rows[0:1, 2048:2560]; r_r = rows[0:1, 2560:3072]

    wo_ready = P.dma("gpsimd", lambda e: e.dma_start(
        out=wo_sb[:].rearrange("p (i n) -> p i n", n=2048),
        in_=wo.rearrange("(i p) n -> p i n", p=128)), "dWo")

    abuf_free = None
    xres_free = [None, None]
    pB_free = None
    pA_free3 = None
    wof_free = [None, None]
    mrep_free = None
    sq_free = None
    x1pre_free = None

    for qt in range(NQT):
        a_ready = P.dma("sync", lambda e, qt=qt: e.dma_start(
            out=abuf[:].rearrange("p (i x) -> p i x", x=512),
            in_=at_sp[:, :, qt * 512:(qt + 1) * 512].transpose([1, 0, 2])),
            "dAbuf", waits=W(abuf_free))
        x1_init = []
        for tc in range(4):
            tb = tc % 2
            xr = P.dma("sync", lambda e, qt=qt, tc=tc, tb=tb: e.dma_start(
                out=xres[tb][:],
                in_=xmine[(qt * 4 + tc) * 128:(qt * 4 + tc + 1) * 128, :]),
                f"dXr{tb}", waits=W(xres_free[tb]))
            for j in range(NJ):
                P.op("tensor", lambda e, j=j, tb=tb: e.transpose(
                    pA[:, j * 128:(j + 1) * 128], xres[tb][:, j * 128:(j + 1) * 128],
                    sid_f[:]),
                    waits=([xr] + W(pA_free3)) if j == 0 else (),
                    inc="sPE" if j == NJ - 1 else None)
            tp = P.ms("sPE")
            xres_free[tb] = tp
            P.op("scalar", lambda e, tc=tc: e.copy(
                x1pre[:].rearrange("p (j x) -> p j x", x=512)[:, :, tc * 128:(tc + 1) * 128],
                pA[:].rearrange("p (j x) -> p j x", x=128)),
                waits=[tp] + (W(x1pre_free) if tc == 0 else []), inc="sSE")
            pA_free3 = P.ms("sSE")
            x1_init.append(pA_free3)
        stats_start = None
        for j in range(NJ):
            jb = j % 2
            for i in range(NJ):
                P.op("tensor", lambda e, i=i, j=j: e.matmul(
                    pB[:], wo_sb[:, i * 2048 + j * 128: i * 2048 + (j + 1) * 128],
                    abuf[:, i * 512:(i + 1) * 512],
                    start=(i == 0), stop=(i == NJ - 1)),
                    waits=([a_ready, wo_ready] + W(pB_free)) if i == 0 else (),
                    inc="sPE" if i == NJ - 1 else None)
            mm = P.ms("sPE")
            P.op("scalar", lambda e, jb=jb: e.copy(wo_f[jb][:], pB[:]),
                 waits=[mm] + W(wof_free[jb]), inc="sSE")
            pB_free = P.ms("sSE")
            P.op("vector", lambda e, j=j, jb=jb: e.tensor_tensor(
                x1pre[:, j * 512:(j + 1) * 512], x1pre[:, j * 512:(j + 1) * 512],
                wo_f[jb][:], ALU.add),
                waits=[pB_free] + (x1_init if j == 0 else []), inc="sVE")
            x1_ms = P.ms("sVE")
            wof_free[jb] = x1_ms
            P.op("vector", lambda e, j=j: e.tensor_tensor(
                sq_t[:], x1pre[:, j * 512:(j + 1) * 512],
                x1pre[:, j * 512:(j + 1) * 512], ALU.mult),
                waits=W(sq_free), inc="sVE")
            sq_ms = P.ms("sVE")
            P.op("tensor", lambda e, j=j: e.matmul(
                pD[0:1, :], mean_c[:], x1pre[:, j * 512:(j + 1) * 512],
                start=(j == 0), stop=(j == NJ - 1)),
                waits=[x1_ms] + (W(mrep_free) if j == 0 else []))
            P.op("tensor", lambda e, j=j: e.matmul(
                pE[0:1, :], mean_c[:], sq_t[:],
                start=(j == 0), stop=(j == NJ - 1)), waits=[sq_ms],
                inc="sPE")
            sq_free = P.ms("sPE")
        stats_done = P.ms("sPE")
        abuf_free = stats_done
        P.op("scalar", lambda e: e.copy(r_m, pD[0:1, :]), waits=[stats_done], inc="sSE")
        P.op("scalar", lambda e: e.copy(r_s, pE[0:1, :]), inc="sSE")
        rows_ms = P.ms("sSE")
        P.op("vector", lambda e: e.tensor_tensor(r_t, r_m, r_m, ALU.mult),
             waits=[rows_ms], inc="sVE")
        P.op("vector", lambda e: e.tensor_tensor(r_s, r_s, r_t, ALU.subtract), inc="sVE")
        P.op("vector", lambda e: e.tensor_scalar(r_u, r_s, 1e-5, None, ALU.add), inc="sVE")
        P.op("vector", lambda e: e.reciprocal(r_t, r_u), inc="sVE")
        P.op("vector", lambda e: e.tensor_tensor(r_q, r_s, r_t, ALU.mult), inc="sVE")
        P.op("vector", lambda e: e.tensor_scalar(r_q, r_q, 1e-6, None, ALU.add), inc="sVE")
        vr = P.ms("sVE")
        P.op("scalar", lambda e: e.activation(r_u, r_u, AF.Ln), waits=[vr], inc="sSE")
        P.op("scalar", lambda e: e.activation(r_r, r_u, AF.Exp, scale=chn[0:1, 0:1]), inc="sSE")
        P.op("scalar", lambda e: e.activation(r_q, r_q, AF.Ln), inc="sSE")
        P.op("scalar", lambda e: e.activation(r_t, r_q, AF.Exp, scale=chn[0:1, 0:1]), inc="sSE")
        P.op("scalar", lambda e, qt=qt: e.activation(
            rsinv_row[0:1, qt * 512:(qt + 1) * 512], r_q, AF.Exp, scale=chp[0:1, 0:1]),
            inc="sSE")
        srow = P.ms("sSE")
        P.op("vector", lambda e: e.tensor_tensor(r_r, r_r, r_t, ALU.mult),
             waits=[srow], inc="sVE")
        rxm = P.ms("sVE")
        P.op("tensor", lambda e: e.matmul(pD[:], onesr[:], r_m, start=True, stop=True),
             waits=[rxm])
        P.op("tensor", lambda e: e.matmul(pE[:], onesr[:], r_r, start=True, stop=True),
             inc="sPE")
        repd = P.ms("sPE")
        P.op("scalar", lambda e: e.copy(mrep_f[:], pD[:]),
             waits=[repd] + W(x1pre_free), inc="sSE")
        P.op("scalar", lambda e: e.copy(rxrep_f[:], pE[:]), inc="sSE")
        repev = P.ms("sSE")
        mrep_free = repev
        for j in range(NJ):
            jb = j % 2
            P.op("vector", lambda e, j=j, jb=jb: e.tensor_tensor(
                wo_f[jb][:], x1pre[:, j * 512:(j + 1) * 512], mrep_f[:], ALU.subtract),
                waits=[repev] + W(wof_free[jb]), inc="sVE")
            P.op("vector", lambda e, j=j, jb=jb, qt=qt: e.tensor_tensor(
                h_sb[:, j * 2048 + qt * 512: j * 2048 + (qt + 1) * 512],
                wo_f[jb][:], rxrep_f[:], ALU.mult), inc="sVE")
            wof_free[jb] = P.ms("sVE")
        x1pre_free = P.ms("sVE")
    h_done = P.ms("sVE")

    P.barrier()

    al = mkalloc("p4", limit=MOE_OFF)
    act = al("act", (128, NFC * 2048), BF16)
    wgu = [al(f"wgu{i}", (128, NJ * 128), BF16) for i in range(3)]
    wdt = [al(f"wdt{i}", (128, NFC * 128), BF16) for i in range(2)]
    wrep = al("wrep", (128, 2048), BF16)
    u_t = al("ut", (128, 2048), BF16)
    d_f = [al(f"df{i}", (128, 512), F32) for i in range(1)]
    rtw_sb = al("rtw", (128, NJ * 8), BF16)
    lg_t = al("lg", (8, 512), F32)
    lgn = al("lgn", (128, 32), F32)
    lge = al("lge", (128, 32), F32)
    weq = al("weq", (128, 32), F32)
    wnat = al("wnat", (128, 32), F32)
    wmax = al("wmax", (128, 4), F32)
    wsum = al("wsum", (128, 4), F32)
    wv1 = al("wv1", (128, 4), F32)
    wv2 = al("wv2", (128, 4), F32)
    wden = al("wden", (128, 4), F32)
    wT_s = al("wTs", (8, TME), BF16)
    wrow = al("wrow", (1, 512), BF16)

    rt_ld = P.dma("gpsimd", lambda e: e.dma_start(
        out=rtw_sb[:].rearrange("p (j n) -> p j n", n=8),
        in_=rtw.rearrange("(j p) n -> p j n", p=128),
        allow_slow_non_contiguous=True), "dRtw")
    for j in range(NJ):
        P.op("vector", lambda e, j=j: e.tensor_scalar(
            rtw_sb[:, j * 8:(j + 1) * 8], rtw_sb[:, j * 8:(j + 1) * 8],
            rw_sb[:, j:j + 1], None, ALU.mult),
            waits=[rt_ld] if j == 0 else (), inc="sVE")
    rt_folded = P.ms("sVE")
    wT_done = []
    lg_free = None
    for qt in range(NQT):
        for j in range(NJ):
            P.op("tensor", lambda e, j=j, qt=qt: e.matmul(
                pD[0:8, :], rtw_sb[:, j * 8:(j + 1) * 8],
                h_sb[:, j * 2048 + qt * 512: j * 2048 + (qt + 1) * 512],
                start=(j == 0), stop=(j == NJ - 1)),
                waits=([rt_folded] + W(lg_free)) if j == 0 else (),
                inc="sPE" if j == NJ - 1 else None)
        lg_done = P.ms("sPE")
        P.op("scalar", lambda e: e.copy(lg_t[:], pD[0:8, :]),
             waits=[lg_done] + W(lg_free), inc="sSE")
        lg_ev = P.ms("sSE")
        for tc in range(4):
            P.op("tensor", lambda e, tc=tc: e.transpose(
                pE[:, tc * 8:(tc + 1) * 8], lg_t[0:8, tc * 128:(tc + 1) * 128], sid_f[0:8, 0:8]),
                waits=[lg_ev] if tc == 0 else (), inc="sPE" if tc == 3 else None)
        tpd = P.ms("sPE")
        P.op("scalar", lambda e: e.copy(lgn[:], pE[:, 0:32]), waits=[tpd], inc="sSE")
        lgn_ev = P.ms("sSE")
        P.op("vector", lambda e: e.tensor_reduce(
            wmax[:], lgn[:].rearrange("p (t x) -> p t x", x=8), AX.X, ALU.max),
            waits=[lgn_ev], inc="sVE")
        for tc in range(4):
            P.op("vector", lambda e, tc=tc: e.tensor_scalar(
                lgn[:, tc * 8:(tc + 1) * 8], lgn[:, tc * 8:(tc + 1) * 8],
                wmax[:, tc:tc + 1], None, ALU.subtract), inc="sVE")
        sub_ms = P.ms("sVE")
        P.op("scalar", lambda e: e.activation(lge[:], lgn[:], AF.Exp),
             waits=[sub_ms], inc="sSE")
        exp_ms = P.ms("sSE")
        P.op("vector", lambda e: e.tensor_reduce(
            wsum[:], lge[:].rearrange("p (t x) -> p t x", x=8), AX.X, ALU.add),
            waits=[exp_ms], inc="sVE")
        P.op("vector", lambda e: e.reciprocal(wsum[:], wsum[:]), inc="sVE")
        for tc in range(4):
            P.op("vector", lambda e, tc=tc: e.tensor_scalar(
                lge[:, tc * 8:(tc + 1) * 8], lge[:, tc * 8:(tc + 1) * 8],
                wsum[:, tc:tc + 1], None, ALU.mult), inc="sVE")
        P.op("vector", lambda e: e.tensor_reduce(
            wv1[:], lge[:].rearrange("p (t x) -> p t x", x=8), AX.X, ALU.max), inc="sVE")
        for tc in range(4):
            P.op("vector", lambda e, tc=tc: e.tensor_scalar(
                weq[:, tc * 8:(tc + 1) * 8], lge[:, tc * 8:(tc + 1) * 8],
                wv1[:, tc:tc + 1], None, ALU.is_equal), inc="sVE")
        P.op("vector", lambda e: e.tensor_scalar(weq[:], weq[:], 1e9, None, ALU.mult),
             inc="sVE")
        P.op("vector", lambda e: e.tensor_tensor(weq[:], lge[:], weq[:], ALU.subtract),
             inc="sVE")
        P.op("vector", lambda e: e.tensor_reduce(
            wv2[:], weq[:].rearrange("p (t x) -> p t x", x=8), AX.X, ALU.max), inc="sVE")
        P.op("vector", lambda e: e.tensor_tensor(wden[:], wv1[:], wv2[:], ALU.add), inc="sVE")
        P.op("vector", lambda e: e.reciprocal(wden[:], wden[:]), inc="sVE")
        for tc in range(4):
            P.op("vector", lambda e, tc=tc: e.tensor_scalar(
                wnat[:, tc * 8:(tc + 1) * 8], lge[:, tc * 8:(tc + 1) * 8],
                wv2[:, tc:tc + 1], None, ALU.is_ge), inc="sVE")
        P.op("vector", lambda e: e.tensor_tensor(wnat[:], wnat[:], lge[:], ALU.mult),
             inc="sVE")
        for tc in range(4):
            P.op("vector", lambda e, tc=tc: e.tensor_scalar(
                wnat[:, tc * 8:(tc + 1) * 8], wnat[:, tc * 8:(tc + 1) * 8],
                wden[:, tc:tc + 1], None, ALU.mult), inc="sVE")
        w_ms = P.ms("sVE")
        for tc in range(4):
            P.op("tensor", lambda e, tc=tc: e.transpose(
                pD[0:8, tc * 128:(tc + 1) * 128], wnat[:, tc * 8:(tc + 1) * 8], sid_f[:]),
                waits=[w_ms, lg_ev] if tc == 0 else (), inc="sPE" if tc == 3 else None)
        wtp = P.ms("sPE")
        P.op("scalar", lambda e, qt=qt: e.copy(
            wT_s[0:8, qt * 512:(qt + 1) * 512], pD[0:8, :]), waits=[wtp], inc="sSE")
        lg_free = P.ms("sSE")
        wT_done.append(lg_free)

    UPB = (pB, pC, pD, pE)
    ucp_free = None
    gu_ld = [None] * 4
    gu_free = [None] * 4
    wdt_free = [None, None]
    wdt_ld = [None, None]
    act_free_fc = [None] * NFC
    ut_free = None
    df_free = [None]
    moe_w = []
    wrep_free = None
    wrow_free = None
    prev_act_ms = None
    pA_free4 = None
    pBC_free = [None, None]

    for eidx in range(9):
        shared = (eidx == 0)
        ex = eidx - 1
        mg = 16 if shared else ex
        mu = 17 if shared else 8 + ex
        wdsrc = swd if shared else wd[ex]
        if not shared:
            for qt in range(NQT):
                wr_ld = P.dma("sync", lambda e, ex=ex, qt=qt: e.dma_start(
                    out=wrow[:], in_=wT_s[ex:ex + 1, qt * 512:(qt + 1) * 512]),
                    "dWrow",
                    waits=(wT_done + W(wrow_free)) if qt == 0 else W(wrow_free))
                P.op("tensor", lambda e: e.matmul(
                    pE[:], onesr_b[:], wrow[:], start=True, stop=True),
                    waits=[wr_ld] + W(wrep_free, ucp_free), inc="sPE")
                wr_mm = P.ms("sPE")
                wrow_free = wr_mm
                P.op("scalar", lambda e, qt=qt: e.copy(
                    wrep[:, qt * 512:(qt + 1) * 512], pE[:]),
                    waits=[wr_mm] + (W(prev_act_ms) if qt == 0 else []), inc="sSE")
                wrep_free = P.ms("sSE")
            wrep_ready = P.ms("sSE")
        else:
            wrep_ready = None
        for fc in range(NFC):
            sg = fc % 2
            su = 2
            g_ready = P.dma("gpsimd", lambda e, mg=mg, fc=fc, sg=sg: e.dma_start(
                out=wgu[sg][:].rearrange("p (j f) -> p j f", f=128),
                in_=wf_sp[mg, :, :, fc * 128:(fc + 1) * 128].transpose([1, 0, 2])),
                f"dGu{sg}", waits=W(gu_free[sg]))
            u_ready = P.dma("gpsimd", lambda e, mu=mu, fc=fc, su=su: e.dma_start(
                out=wgu[su][:].rearrange("p (j f) -> p j f", f=128),
                in_=wf_sp[mu, :, :, fc * 128:(fc + 1) * 128].transpose([1, 0, 2])),
                f"dGu{su}", waits=W(gu_free[su]))
            gu_ready = [g_ready, u_ready]
            for j in range(NJ):
                for qt in range(NQT):
                    P.op("tensor", lambda e, j=j, qt=qt, sg=sg: e.matmul(
                        pA[:, qt * 512:(qt + 1) * 512],
                        wgu[sg][:, j * 128:(j + 1) * 128],
                        h_sb[:, j * 2048 + qt * 512: j * 2048 + (qt + 1) * 512],
                        start=(j == 0), stop=(j == NJ - 1)),
                        waits=([g_ready] + W(pA_free4)) if (j == 0 and qt == 0) else (),
                        inc="sPE" if (j == NJ - 1 and qt == NQT - 1) else None)
            g_done = P.ms("sPE")
            P.op("scalar", lambda e, fc=fc: e.activation(
                act[:, fc * 2048:(fc + 1) * 2048], pA[:], AF.Silu),
                waits=[g_done] + W(act_free_fc[fc]), inc="sSE")
            pA_free4 = P.ms("sSE")
            up_w0 = [u_ready] + W(ucp_free)
            if fc == 0:
                up_w0 += wT_done if shared else [wrep_ready]
                up_w0 += W(pBC_free[0], pBC_free[1])
            for j in range(NJ):
                for qt in range(NQT):
                    P.op("tensor", lambda e, j=j, qt=qt, su=su: e.matmul(
                        UPB[qt][:],
                        wgu[su][:, j * 128:(j + 1) * 128],
                        h_sb[:, j * 2048 + qt * 512: j * 2048 + (qt + 1) * 512],
                        start=(j == 0), stop=(j == NJ - 1)),
                        waits=up_w0 if (j == 0 and qt == 0) else (),
                        inc="sPE" if (j == NJ - 1 and qt == NQT - 1) else None)
            u_done = P.ms("sPE")
            gu_free[sg] = u_done
            gu_free[su] = u_done
            for qt in range(NQT):
                P.op("scalar", lambda e, qt=qt: e.copy(
                    u_t[:, qt * 512:(qt + 1) * 512], UPB[qt][:]),
                    waits=([u_done] + W(ut_free)) if qt == 0 else (), inc="sSE")
            ucp_free = P.ms("sSE")
            P.op("vector", lambda e, fc=fc: e.tensor_tensor(
                act[:, fc * 2048:(fc + 1) * 2048], act[:, fc * 2048:(fc + 1) * 2048],
                u_t[:], ALU.mult), waits=[ucp_free], inc="sVE")
            if not shared:
                P.op("vector", lambda e, fc=fc: e.tensor_tensor(
                    act[:, fc * 2048:(fc + 1) * 2048], act[:, fc * 2048:(fc + 1) * 2048],
                    wrep[:], ALU.mult), waits=[wrep_ready], inc="sVE")
            act_ms = P.ms("sVE")
            ut_free = act_ms
            act_free_fc[fc] = None
        act_ready = P.ms("sVE")
        for od in range(NJ):
            ob = od % 2
            wdt_ld[ob] = P.dma("gpsimd", lambda e, od=od, ob=ob, wdsrc=wdsrc: e.dma_start(
                out=wdt[ob][:].rearrange("p (fc o) -> p fc o", o=128),
                in_=wdsrc[:, od * 128:(od + 1) * 128].rearrange("(fc p) o -> p fc o", p=128)),
                f"dWdt{ob}", waits=W(wdt_free[ob]))
            for qt in range(NQT):
                pp = pB if (od * NQT + qt) % 2 == 0 else pC
                ppi = (od * NQT + qt) % 2
                for fc in range(NFC):
                    P.op("tensor", lambda e, fc=fc, qt=qt, ob=ob, pp=pp: e.matmul(
                        pp[:], wdt[ob][:, fc * 128:(fc + 1) * 128],
                        act[:, fc * 2048 + qt * 512: fc * 2048 + (qt + 1) * 512],
                        start=(fc == 0), stop=(fc == NFC - 1)),
                        waits=([wdt_ld[ob], act_ready] + W(pBC_free[ppi]))
                              if fc == 0 else (),
                        inc="sPE" if fc == NFC - 1 else None)
                dmm = P.ms("sPE")
                if qt == NQT - 1:
                    wdt_free[ob] = dmm
                msl = moe[:, od * 2048 + qt * 512: od * 2048 + (qt + 1) * 512]
                if shared:
                    wmoe = [dmm]
                    if od == 0 and qt == 0:
                        wmoe = wmoe + wT_done
                    P.op("scalar", lambda e, msl=msl, pp=pp: e.copy(msl, pp[:]),
                         waits=wmoe, inc="sSE")
                    pBC_free[ppi] = P.ms("sSE")
                    moe_w.append(pBC_free[ppi])
                else:
                    P.op("scalar", lambda e, pp=pp: e.copy(d_f[0][:], pp[:]),
                         waits=[dmm] + W(df_free[0]), inc="sSE")
                    pBC_free[ppi] = P.ms("sSE")
                    dev = pBC_free[ppi]
                    P.op("vector", lambda e, msl=msl: e.tensor_tensor(
                        msl, msl, d_f[0][:], ALU.add), waits=[dev], inc="sVE")
                    df_free[0] = P.ms("sVE")
                    moe_w.append(df_free[0])
        alldown = P.ms("sPE")
        for fc in range(NFC):
            act_free_fc[fc] = alldown
        prev_act_ms = P.ms("sVE")

    P.barrier()

    al = mkalloc("p5", limit=MOE_OFF)
    x2pre = al("x2pre", (128, NJ * 512), BF16)
    x2f = al("x2f", (128, NJ * 512), BF16)
    xout = al("xout", (128, 2048), F32)
    t_f = [al(f"tf{i}", (128, 512), F32) for i in range(2)]
    rsrep_f = al("rsrep", (128, 512), F32)
    m2rep_f = al("m2rep", (128, 512), F32)
    r2rep_f = al("r2rep", (128, 512), F32)
    rows5 = al("rows5", (1, 4 * 512), F32)
    q_m = rows5[0:1, 0:512]; q_s = rows5[0:1, 512:1024]
    q_t = rows5[0:1, 1024:1536]; q_u = rows5[0:1, 1536:2048]
    sq5 = al("sq5", (128, 512), BF16)

    tf_free = [None, None]
    xout_free = None
    pA_free5 = None
    rep_free5 = None
    x2pre_prev = None
    sq5_free = None
    out_dmas = []

    for qt in range(NQT):
        P.op("tensor", lambda e, qt=qt: e.matmul(
            pD[:], onesr[:], rsinv_row[0:1, qt * 512:(qt + 1) * 512],
            start=True, stop=True), waits=W(rep_free5), inc="sPE")
        rsd = P.ms("sPE")
        P.op("scalar", lambda e: e.copy(rsrep_f[:], pD[:]),
             waits=[rsd] + W(x2pre_prev), inc="sSE")
        rsr = P.ms("sSE")
        for j in range(NJ):
            jb = j % 2
            P.op("vector", lambda e, j=j, jb=jb, qt=qt: e.tensor_tensor(
                t_f[jb][:], h_sb[:, j * 2048 + qt * 512: j * 2048 + (qt + 1) * 512],
                rsrep_f[:], ALU.mult), waits=[rsr] + W(tf_free[jb]), inc="sVE")
            P.op("vector", lambda e, j=j, jb=jb, qt=qt: e.tensor_tensor(
                x2pre[:, j * 512:(j + 1) * 512], t_f[jb][:],
                moe[:, j * 2048 + qt * 512: j * 2048 + (qt + 1) * 512], ALU.add),
                inc="sVE")
            x2_ms = P.ms("sVE")
            tf_free[jb] = x2_ms
            P.op("vector", lambda e, j=j: e.tensor_tensor(
                sq5[:], x2pre[:, j * 512:(j + 1) * 512],
                x2pre[:, j * 512:(j + 1) * 512], ALU.mult),
                waits=W(sq5_free), inc="sVE")
            sq_ms = P.ms("sVE")
            P.op("tensor", lambda e, j=j: e.matmul(
                pD[0:1, :], mean_c[:], x2pre[:, j * 512:(j + 1) * 512],
                start=(j == 0), stop=(j == NJ - 1)), waits=[x2_ms, rsd])
            P.op("tensor", lambda e, j=j: e.matmul(
                pE[0:1, :], mean_c[:], sq5[:],
                start=(j == 0), stop=(j == NJ - 1)), waits=[sq_ms],
                inc="sPE")
            sq5_free = P.ms("sPE")
        st_done = P.ms("sPE")
        P.op("scalar", lambda e: e.copy(q_m, pD[0:1, :]), waits=[st_done], inc="sSE")
        P.op("scalar", lambda e: e.copy(q_s, pE[0:1, :]), inc="sSE")
        rws = P.ms("sSE")
        P.op("vector", lambda e: e.tensor_tensor(q_t, q_m, q_m, ALU.mult),
             waits=[rws], inc="sVE")
        P.op("vector", lambda e: e.tensor_tensor(q_s, q_s, q_t, ALU.subtract), inc="sVE")
        P.op("vector", lambda e: e.tensor_scalar(q_u, q_s, 1e-5, None, ALU.add), inc="sVE")
        vr5 = P.ms("sVE")
        P.op("scalar", lambda e: e.activation(q_u, q_u, AF.Ln), waits=[vr5], inc="sSE")
        P.op("scalar", lambda e: e.activation(q_t, q_u, AF.Exp, scale=chn[0:1, 0:1]),
             inc="sSE")
        sr5 = P.ms("sSE")
        P.op("tensor", lambda e: e.matmul(pD[:], onesr[:], q_m, start=True, stop=True),
             waits=[sr5])
        P.op("tensor", lambda e: e.matmul(pE[:], onesr[:], q_t, start=True, stop=True),
             inc="sPE")
        rp5 = P.ms("sPE")
        P.op("scalar", lambda e: e.copy(m2rep_f[:], pD[:]),
             waits=[rp5] + W(x2pre_prev), inc="sSE")
        P.op("scalar", lambda e: e.copy(r2rep_f[:], pE[:]), inc="sSE")
        rpe5 = P.ms("sSE")
        rep_free5 = rpe5
        for j in range(NJ):
            jb = j % 2
            P.op("vector", lambda e, j=j, jb=jb: e.tensor_tensor(
                t_f[jb][:], x2pre[:, j * 512:(j + 1) * 512], m2rep_f[:], ALU.subtract),
                waits=[rpe5] + W(tf_free[jb]), inc="sVE")
            P.op("vector", lambda e, j=j, jb=jb: e.tensor_tensor(
                x2f[:, j * 512:(j + 1) * 512], t_f[jb][:], r2rep_f[:], ALU.mult),
                inc="sVE")
            tf_free[jb] = P.ms("sVE")
        x2f_done = P.ms("sVE")
        x2pre_prev = x2f_done
        for tc in range(4):
            for j in range(NJ):
                P.op("tensor", lambda e, j=j, tc=tc: e.transpose(
                    pA[:, j * 64:(j + 1) * 64].bitcast(BF16),
                    x2f[:, j * 512 + tc * 128: j * 512 + (tc + 1) * 128], sid_b[:]),
                    waits=([x2f_done] + W(pA_free5)) if j == 0 else (),
                    inc="sPE" if j == NJ - 1 else None)
            otp = P.ms("sPE")
            P.op("scalar", lambda e: e.copy(xout[:], pA[:, 0:1024].bitcast(BF16)),
                 waits=[otp] + W(xout_free), inc="sSE")
            pA_free5 = P.ms("sSE")
            xout_free = P.dma("sync", lambda e, qt=qt, tc=tc: e.dma_start(
                out=dout[(qt * 4 + tc) * 128:(qt * 4 + tc + 1) * 128, :], in_=xout[:]),
                "dOut", waits=[pA_free5])
            out_dmas.append(xout_free)

    P.barrier()

    with ExitStack() as stack:
        sems = {s: stack.enter_context(nc.semaphore(s)) for s in P.cnt}
        with nc.Block() as block:
            def runner(eng_name):
                def run(eng):
                    lastw = {}
                    for waits, fn, inc, amt in P.ops[eng_name]:
                        for sem, val in waits:
                            if lastw.get(sem, -1) < val:
                                eng.wait_ge(sems[sem], val)
                                lastw[sem] = val
                        if fn is None:
                            continue
                        instr = fn(eng)
                        if inc is not None:
                            instr.then_inc(sems[inc], amt)
                return run
            block.tensor(runner("tensor"))
            block.scalar(runner("scalar"))
            block.vector(runner("vector"))
            block.sync(runner("sync"))
            block.gpsimd(runner("gpsimd"))
    return nc


_NC_CACHE = {}


def kernel(L0_attn_hidden_states, L0_attn_Wq, L0_attn_Wk, L0_attn_Wv, L0_attn_Wo,
           L0_ffn_rms_weight, L0_ffn_router_w, L0_ffn_w_gate, L0_ffn_w_up,
           L0_ffn_w_down, L0_ffn_sw_gate, L0_ffn_sw_up, L0_ffn_sw_down):
    x = np.ascontiguousarray(np.asarray(L0_attn_hidden_states, dtype=np.float32))
    perm = np.concatenate([np.arange(0, HD, 2), np.arange(1, HD, 2)])

    def permute_cols(w, nh):
        w = np.asarray(w, dtype=np.float32)
        return np.ascontiguousarray(w.reshape(D, nh, HD)[:, :, perm].reshape(D, nh * HD))

    wq_p = permute_cols(L0_attn_Wq, NH)
    wk_p = permute_cols(L0_attn_Wk, NKV)
    wv = np.ascontiguousarray(np.asarray(L0_attn_Wv, dtype=np.float32))
    wo = np.ascontiguousarray(np.asarray(L0_attn_Wo, dtype=np.float32))
    rw = np.ascontiguousarray(np.asarray(L0_ffn_rms_weight, dtype=np.float32).reshape(D, 1))
    rtw = np.ascontiguousarray(np.asarray(L0_ffn_router_w, dtype=np.float32))
    wg = np.ascontiguousarray(np.asarray(L0_ffn_w_gate, dtype=np.float32))
    wu = np.ascontiguousarray(np.asarray(L0_ffn_w_up, dtype=np.float32))
    wd = np.ascontiguousarray(np.asarray(L0_ffn_w_down, dtype=np.float32))
    swg = np.ascontiguousarray(np.asarray(L0_ffn_sw_gate, dtype=np.float32))
    swu = np.ascontiguousarray(np.asarray(L0_ffn_sw_up, dtype=np.float32))
    swd = np.ascontiguousarray(np.asarray(L0_ffn_sw_down, dtype=np.float32))

    inv = 1.0 / THETA ** (np.arange(0, HD, 2, dtype=np.float64) / HD)
    tpos = np.arange(S, dtype=np.float64)
    fr = np.outer(tpos, inv)
    Cb, Sb = np.cos(fr), np.sin(fr)
    cc1 = np.concatenate([Cb, Cb], axis=1)
    ss1 = np.concatenate([-Sb, Sb], axis=1)
    rkc = np.ascontiguousarray(np.tile(cc1, (1, NKV)).astype(np.float32))
    rks = np.ascontiguousarray(np.tile(ss1, (1, NKV)).astype(np.float32))
    scale = HD ** -0.5
    rqc_full = np.tile(cc1 * scale, (1, NH)).astype(np.float32)
    rqs_full = np.tile(ss1 * scale, (1, NH)).astype(np.float32)
    idb = np.eye(128, dtype=ml_dtypes.bfloat16)
    idf = np.eye(128, dtype=np.float32)

    nc = build_nc()
    in_maps = []
    for c in range(8):
        b = c // 4
        qoff = (c % 4) * TME
        in_maps.append({
            "xfull": x[b], "xmine": np.ascontiguousarray(x[b, qoff:qoff + TME]),
            "wq": wq_p, "wk": wk_p, "wv": wv, "wo": wo, "rw": rw, "rtw": rtw,
            "wg": wg, "wu": wu, "wd": wd, "swg": swg, "swu": swu, "swd": swd,
            "rkc": rkc, "rks": rks,
            "rqc": np.ascontiguousarray(rqc_full[qoff:qoff + TME]),
            "rqs": np.ascontiguousarray(rqs_full[qoff:qoff + TME]),
            "idb": idb, "idf": idf,
        })
    res = run_bass_kernel_spmd(nc, in_maps, core_ids=list(range(8)))
    out = np.empty((2, S, D), dtype=np.float32)
    for c in range(8):
        b = c // 4
        qoff = (c % 4) * TME
        out[b, qoff:qoff + TME] = np.asarray(res.results[c]["out"], dtype=np.float32)
    return out


# version 18 (final)
# speedup vs baseline: 1.0636x; 1.0636x over previous
import numpy as np
import ml_dtypes
import concourse.bass as bass
import concourse.mybir as mybir
from concourse.bass_utils import run_bass_kernel_spmd
from contextlib import ExitStack

F32 = mybir.dt.float32
BF16 = mybir.dt.bfloat16
AF = mybir.ActivationFunctionType
ALU = mybir.AluOpType
AX = mybir.AxisListType

S, D, HD, NH, NKV = 8192, 2048, 128, 16, 4
KVD = NKV * HD
TME = 2048
NCH = S // 128
NMY = TME // 128
NJ = D // 128
NQT = TME // 512
E, F, SF = 8, 1024, 1024
NFC = F // 128
SHIFT = 12.0
THETA = 10000.0
SBUF_BASE = 16576
SBUF_TOP = 229312
SBUF_BYTES = SBUF_TOP

ENGS = ("tensor", "scalar", "vector", "sync", "gpsimd")
OWN = {"tensor": "sPE", "scalar": "sSE", "vector": "sVE", "gpsimd": "sPO"}
SELFSYNC = ("scalar", "vector", "gpsimd")


class Plan:
    def __init__(self):
        self.ops = {e: [] for e in ENGS}
        self.cnt = {"sPE": 0, "sSE": 0, "sVE": 0, "sPO": 0}

    def op(self, eng, fn, waits=(), inc=None):
        waits = [w for w in waits if w is not None]
        if eng in SELFSYNC:
            inc = OWN[eng]
            waits.append((inc, self.cnt[inc]))
        if inc is not None:
            self.cnt[inc] += 1
        self.ops[eng].append((tuple(waits), fn, inc, 1))
        return (inc, self.cnt[inc]) if inc else None

    def dma(self, eng, fn, stream, waits=()):
        waits = [w for w in waits if w is not None]
        self.cnt[stream] = self.cnt.get(stream, 0) + 16
        self.ops[eng].append((tuple(waits), fn, stream, 16))
        return (stream, self.cnt[stream])

    def ms(self, sem):
        return (sem, self.cnt.get(sem, 0))

    def barrier(self):
        for eng in ("tensor", "scalar", "vector", "gpsimd"):
            s = OWN[eng]
            self.op(eng, lambda e: e.nop(), waits=[(s, self.cnt[s])], inc=s)
        counts = list(self.cnt.items())
        for eng in ENGS:
            self.ops[eng].append((tuple(counts), None, None, 0))


def build_nc():
    nc = bass.Bass("TRN2", target_bir_lowering=False)
    din = {}

    def dI(name, shape, dt=F32):
        din[name] = nc.dram_tensor(name, list(shape), dt, kind="ExternalInput").ap()
        return din[name]

    xfull = dI("xfull", (S, D))
    xmine = dI("xmine", (TME, D))
    wq = dI("wq", (D, D)); wk = dI("wk", (D, KVD)); wv = dI("wv", (D, KVD))
    wo = dI("wo", (D, D))
    rw = dI("rw", (D, 1))
    rtw = dI("rtw", (D, E))
    wg = dI("wg", (E, D, F)); wu = dI("wu", (E, D, F)); wd = dI("wd", (E, F, D))
    swg = dI("swg", (D, SF)); swu = dI("swu", (D, SF)); swd = dI("swd", (SF, D))
    rkc = dI("rkc", (S, KVD)); rks = dI("rks", (S, KVD))
    rqc = dI("rqc", (TME, D)); rqs = dI("rqs", (TME, D))
    idb = dI("idb", (128, 128), BF16)
    idf = dI("idf", (128, 128), F32)
    dout = nc.dram_tensor("out", [TME, D], F32, kind="ExternalOutput").ap()

    kr_sp = nc.dram_tensor("kr_sp", [S, KVD], BF16, kind="Internal").ap()
    v_sp = nc.dram_tensor("v_sp", [NCH, 128, KVD], BF16, kind="Internal").ap()
    qr_sp = nc.dram_tensor("qr_sp", [TME, D], BF16, kind="Internal").ap()
    at_sp = nc.dram_tensor("at_sp", [NH, 128, TME], BF16, kind="Internal").ap()
    wf_sp = nc.dram_tensor("wf_sp", [18, NJ, 128, F], BF16, kind="Internal").ap()

    H_OFF = SBUF_TOP - 65536
    MOE_OFF = H_OFF - 65536
    galloc_off = [SBUF_BASE]

    def galloc(name, shape, dt, align=64):
        nb = {F32: 4, BF16: 2}[dt] * int(np.prod(shape[1:]))
        o = (galloc_off[0] + align - 1) // align * align
        galloc_off[0] = o + nb
        return nc.alloc_sbuf_tensor_at(name, list(shape), dt, offset=o)

    sid_b = galloc("sid_b", (128, 128), BF16)
    sid_f = galloc("sid_f", (128, 128), F32)
    ones_c = galloc("ones_c", (128, 1), BF16)
    ones_cf = galloc("ones_cf", (128, 1), F32)
    mean_c = galloc("mean_c", (128, 1), BF16)
    onesr = galloc("onesr", (1, 128), F32)
    onesr_b = galloc("onesr_b", (1, 128), BF16)
    cexpb = galloc("cexpb", (128, 1), F32)
    chn = galloc("chn", (1, 1), F32)
    chp = galloc("chp", (1, 1), F32)
    rw_sb = galloc("rw_sb", (128, NJ), F32)
    rsinv_row = galloc("rsinv_row", (1, TME), F32)
    GEND = (galloc_off[0] + 63) // 64 * 64

    def mkalloc(tag, limit=SBUF_TOP):
        st = [GEND]

        def alloc(name, shape, dt, align=64):
            nb = {F32: 4, BF16: 2}[dt] * int(np.prod(shape[1:]))
            o = (st[0] + align - 1) // align * align
            st[0] = o + nb
            assert st[0] <= limit, (tag, name, st[0], limit)
            return nc.alloc_sbuf_tensor_at(f"{tag}_{name}", list(shape), dt, offset=o)
        return alloc

    h_sb = nc.alloc_sbuf_tensor_at("h_sb", [128, NJ * 2048], BF16, offset=H_OFF)
    moe = nc.alloc_sbuf_tensor_at("moe", [128, NJ * 2048], BF16, offset=MOE_OFF)

    pA = nc.alloc_psum_tensor("pA", [128, 2048], F32)
    pB = nc.alloc_psum_tensor("pB", [128, 512], F32)
    pC = nc.alloc_psum_tensor("pC", [128, 512], F32)
    pD = nc.alloc_psum_tensor("pD", [128, 512], F32)
    pE = nc.alloc_psum_tensor("pE", [128, 512], F32)

    P = Plan()
    W = lambda *ms: [m for m in ms if m is not None]

    P.dma("sync", lambda e: e.dma_start(out=sid_b[:], in_=idb), "dId")
    id_ready = P.dma("sync", lambda e: e.dma_start(out=sid_f[:], in_=idf), "dId")
    rw_ld = P.dma("sync", lambda e: e.dma_start(
        out=rw_sb[:], in_=rw.rearrange("(j p) o -> p (j o)", p=128),
        allow_slow_non_contiguous=True), "dRw")
    P.op("vector", lambda e: e.memset(ones_c[:], 1.0))
    P.op("vector", lambda e: e.memset(ones_cf[:], 1.0))
    P.op("vector", lambda e: e.memset(mean_c[:], 1.0 / 2048.0))
    P.op("vector", lambda e: e.memset(onesr[:], 1.0))
    P.op("vector", lambda e: e.memset(onesr_b[:], 1.0))
    P.op("vector", lambda e: e.memset(cexpb[:], -SHIFT))
    P.op("vector", lambda e: e.memset(chn[:], -0.5))
    P.op("vector", lambda e: e.memset(chp[:], 0.5), inc="sVE")
    init_ve = P.ms("sVE")

    al = mkalloc("p1")
    wq_sb = al("wq", (128, NJ * 2048), BF16)
    wk_sb = al("wk", (128, NJ * 512), BF16)
    wv_sb = al("wv", (128, NJ * 512), BF16)
    xbf = [al(f"xbf{i}", (128, 2048), BF16) for i in range(2)]
    xt = [al(f"xt{i}", (128, 2048), BF16) for i in range(2)]
    kf = al("kf", (128, 512), F32)
    km1 = al("km1", (128, 512), F32)
    km2 = al("km2", (128, 512), F32)
    krot = [al(f"krot{i}", (128, 512), BF16) for i in range(2)]
    vt = [al(f"vt{i}", (128, 512), BF16) for i in range(2)]
    tbc = al("tbc", (128, 512), F32)
    tbs = al("tbs", (128, 512), F32)
    xf32 = [al(f"xf32{i}", (128, 2048), F32) for i in range(2)]
    qf = al("qf", (128, 2048), F32)
    qm1 = al("qm1", (128, 2048), F32)
    qm2 = al("qm2", (128, 512), F32)
    qrot = al("qrot", (128, 2048), BF16)
    wtmp = [al(f"wtmp{i}", (128, NJ * 128), BF16) for i in range(3)]
    rmf = al("rmf", (128, NJ * 128), BF16)

    P.op("vector", lambda e: e.memset(rmf[:], 1.0), inc="sVE")
    for j in range(NJ):
        P.op("vector", lambda e, j=j: e.tensor_scalar(
            rmf[:, j * 128:(j + 1) * 128], rmf[:, j * 128:(j + 1) * 128],
            rw_sb[:, j:j + 1], None, ALU.mult),
            waits=[rw_ld] if j == 0 else (), inc="sVE")
    rmf_ready = P.ms("sVE")

    P.dma("gpsimd", lambda e: e.dma_start(
        out=wk_sb[:].rearrange("p (j n) -> p j n", n=512),
        in_=wk.rearrange("(j p) n -> p j n", p=128)), "dWqkv")
    P.dma("gpsimd", lambda e: e.dma_start(
        out=wv_sb[:].rearrange("p (j n) -> p j n", n=512),
        in_=wv.rearrange("(j p) n -> p j n", p=128)), "dWqkv")
    w_ready = P.dma("gpsimd", lambda e: e.dma_start(
        out=wq_sb[:].rearrange("p (j n) -> p j n", n=2048),
        in_=wq.rearrange("(j p) n -> p j n", p=128)), "dWqkv")

    fold_jobs = []
    for m in range(18):
        src = wg[m] if m < 8 else (wu[m - 8] if m < 16 else (swg if m == 16 else swu))
        for sl in range(8):
            fold_jobs.append((m, sl, src))
    fold_i = [0]
    fold_busy = [None, None, None]

    def fold_step():
        i = fold_i[0]
        if i >= len(fold_jobs):
            return
        fold_i[0] = i + 1
        m, sl, src = fold_jobs[i]
        buf = i % 3
        wt = wtmp[buf]
        ld = P.dma("gpsimd", lambda e, wt=wt, src=src, sl=sl: e.dma_start(
            out=wt[:].rearrange("p (j f) -> p j f", f=128),
            in_=src[:, sl * 128:(sl + 1) * 128].rearrange("(j p) f -> p j f", p=128)),
            f"dWtmp{buf}", waits=W(fold_busy[buf]))
        folded = P.op("vector", lambda e, wt=wt: e.tensor_tensor(
            wt[:], wt[:], rmf[:], ALU.mult), waits=[ld, rmf_ready])
        fold_busy[buf] = P.dma("sync", lambda e, wt=wt, m=m, sl=sl: e.dma_start(
            out=wf_sp[m, :, :, sl * 128:(sl + 1) * 128].transpose([1, 0, 2]),
            in_=wt[:].rearrange("p (j f) -> p j f", f=128)),
            f"dFold{buf}", waits=[folded])

    xbf_free = [None, None]
    xt_free = [None, None]
    pD_free = None
    pE_free = None
    pK_free = None
    pV_free = None
    pA_free = None
    kf_free = None
    qf_free = None
    qrot_free = [None]
    krot_free = [None, None]
    vt_free = [None, None]
    tb_free = None
    kt_spilled = None
    qt_spilled = None
    strip_evs = []
    qstrip_evs = []

    pA_free_p1 = [None]

    xt_rdy = {}
    cast_rdy = {}
    xf32_free = [None, None]

    def load_cast(c):
        mine = c >= NCH
        cc = c - NCH if mine else c
        buf = c % 2
        src = (xmine if mine else xfull)[cc * 128:(cc + 1) * 128, :]
        x_ld = P.dma("scalar", lambda e, b=buf, src=src: e.dma_start(
            out=xf32[b][:], in_=src), f"dXf{buf}", waits=W(xf32_free[buf]))
        P.op("vector", lambda e, b=buf: e.tensor_copy(xbf[b][:], xf32[b][:]),
             waits=[x_ld] + W(xbf_free[buf]))
        cast_rdy[buf] = P.ms("sVE")
        xf32_free[buf] = cast_rdy[buf]

    def prep_t(c):
        nonlocal pD_free, pE_free
        buf = c % 2
        for j in range(NJ):
            tgt, colo = (pD, j * 64) if j < 8 else (pE, (j - 8) * 64)
            wt0 = W(cast_rdy[buf], pD_free, pE_free, init_ve, id_ready) if j == 0 else ()
            P.op("tensor", lambda e, b=buf, j=j, tgt=tgt, colo=colo: e.transpose(
                tgt[:, colo:colo + 64].bitcast(BF16),
                xbf[b][:, j * 128:(j + 1) * 128], sid_b[:]),
                waits=wt0, inc="sPE" if j in (7, 15) else None)
            if j == 7:
                tpD = P.ms("sPE")
        tpE = P.ms("sPE")
        xbf_free[buf] = tpE
        P.op("scalar", lambda e, b=buf: e.copy(xt[b][:, 0:1024], pD[:].bitcast(BF16)),
             waits=[tpD] + W(xt_free[buf]), inc="sSE")
        xt_r1 = P.ms("sSE")
        P.op("scalar", lambda e, b=buf: e.copy(xt[b][:, 1024:2048], pE[:].bitcast(BF16)),
             waits=[tpE], inc="sSE")
        xt_ready = P.ms("sSE")
        pD_free = xt_ready
        pE_free = xt_ready
        xt_rdy[buf] = (xt_r1, xt_ready)

    load_cast(0)
    load_cast(1)
    prep_t(0)
    for c in range(NCH + NMY):
        mine = c >= NCH
        cc = c - NCH if mine else c
        buf = c % 2
        xt_r1, xt_ready = xt_rdy[buf]

        if not mine:
            for j in range(NJ):
                wj = ()
                if j == 0:
                    wj = [xt_r1, w_ready] + W(pK_free)
                elif j == 8:
                    wj = [xt_ready]
                P.op("tensor", lambda e, b=buf, j=j: e.matmul(
                    pB[:], xt[b][:, j * 128:(j + 1) * 128],
                    wk_sb[:, j * 512:(j + 1) * 512],
                    start=(j == 0), stop=(j == NJ - 1)),
                    waits=wj, inc="sPE" if j == NJ - 1 else None)
            k_done = P.ms("sPE")
            if c + 2 < NCH + NMY:
                load_cast(c + 2)
            if c + 1 < NCH + NMY:
                prep_t(c + 1)
            for j in range(NJ):
                P.op("tensor", lambda e, b=buf, j=j: e.matmul(
                    pC[:], xt[b][:, j * 128:(j + 1) * 128],
                    wv_sb[:, j * 512:(j + 1) * 512],
                    start=(j == 0), stop=(j == NJ - 1)),
                    waits=W(pV_free) if j == 0 else (),
                    inc="sPE" if j == NJ - 1 else None)
            kv_done = P.ms("sPE")
            xt_free[buf] = kv_done
            P.op("scalar", lambda e: e.copy(kf[:], pB[:]),
                 waits=[k_done] + W(kf_free), inc="sSE")
            kf_ready = P.ms("sSE")
            pK_free = kf_ready
            P.op("scalar", lambda e, b=buf: e.copy(vt[b][:], pC[:]),
                 waits=[kv_done] + W(vt_free[buf]), inc="sSE")
            v_ready = P.ms("sSE")
            pV_free = v_ready
            vt_free[buf] = P.dma("sync", lambda e, b=buf, c=c: e.dma_start(
                out=v_sp[c], in_=vt[b][:]), f"dVsp{buf}", waits=[v_ready])
            P.dma("sync", lambda e, cc=cc: e.dma_start(
                out=tbc[:], in_=rkc[cc * 128:(cc + 1) * 128, :]),
                "dTb", waits=W(tb_free))
            tb_ready = P.dma("sync", lambda e, cc=cc: e.dma_start(
                out=tbs[:], in_=rks[cc * 128:(cc + 1) * 128, :]), "dTb")
            P.op("vector", lambda e: e.tensor_tensor(km1[:], kf[:], tbc[:], ALU.mult),
                 waits=[kf_ready, tb_ready], inc="sVE")
            P.op("vector", lambda e: e.tensor_tensor(
                km2[:].rearrange("p (h s x) -> p h s x", h=4, s=2)[:, :, 0, :],
                kf[:].rearrange("p (h s x) -> p h s x", h=4, s=2)[:, :, 1, :],
                tbs[:].rearrange("p (h s x) -> p h s x", h=4, s=2)[:, :, 0, :],
                ALU.mult), inc="sVE")
            P.op("vector", lambda e: e.tensor_tensor(
                km2[:].rearrange("p (h s x) -> p h s x", h=4, s=2)[:, :, 1, :],
                kf[:].rearrange("p (h s x) -> p h s x", h=4, s=2)[:, :, 0, :],
                tbs[:].rearrange("p (h s x) -> p h s x", h=4, s=2)[:, :, 1, :],
                ALU.mult), inc="sVE")
            P.op("vector", lambda e, b=buf: e.tensor_tensor(
                krot[b][:], km1[:], km2[:], ALU.add),
                waits=W(krot_free[buf]), inc="sVE")
            rope_done = P.ms("sVE")
            kf_free = rope_done
            tb_free = rope_done
            krot_free[buf] = P.dma("sync", lambda e, b=buf, c=c: e.dma_start(
                out=kr_sp[c * 128:(c + 1) * 128, :], in_=krot[b][:]),
                f"dKr{buf}", waits=[rope_done])
        else:
            for qb in range(4):
                for j in range(NJ):
                    P.op("tensor", lambda e, b=buf, j=j, qb=qb: e.matmul(
                        pA[:, qb * 512:(qb + 1) * 512],
                        xt[b][:, j * 128:(j + 1) * 128],
                        wq_sb[:, j * 2048 + qb * 512: j * 2048 + (qb + 1) * 512],
                        start=(j == 0), stop=(j == NJ - 1)),
                        waits=([xt_r1, w_ready] + W(pA_free_p1[0]))
                              if (qb == 0 and j == 0) else
                              ([xt_ready] if (qb == 0 and j == 8) else ()),
                        inc="sPE" if (qb == 3 and j == NJ - 1) else None)
            q_done = P.ms("sPE")
            xt_free[buf] = q_done
            P.op("scalar", lambda e: e.copy(qf[:, 0:1024], pA[:, 0:1024]),
                 waits=[q_done] + W(qf_free), inc="sSE")
            P.op("scalar", lambda e: e.copy(qf[:, 1024:2048], pA[:, 1024:2048]), inc="sSE")
            qf_ready = P.ms("sSE")
            pA_free_p1[0] = qf_ready
            if c + 2 < NCH + NMY:
                load_cast(c + 2)
            if c + 1 < NCH + NMY:
                prep_t(c + 1)
            last_ms = None
            for si in range(4):
                P.dma("sync", lambda e, cc=cc, si=si: e.dma_start(
                    out=tbc[:], in_=rqc[cc * 128:(cc + 1) * 128, si * 512:(si + 1) * 512]),
                    "dTb", waits=W(tb_free))
                tb_ready = P.dma("sync", lambda e, cc=cc, si=si: e.dma_start(
                    out=tbs[:], in_=rqs[cc * 128:(cc + 1) * 128, si * 512:(si + 1) * 512]),
                    "dTb")
                sl = slice(si * 512, (si + 1) * 512)
                P.op("vector", lambda e, sl=sl: e.tensor_tensor(
                    qm1[:, sl], qf[:, sl], tbc[:], ALU.mult),
                    waits=[qf_ready, tb_ready], inc="sVE")
                P.op("vector", lambda e, sl=sl: e.tensor_tensor(
                    qm2[:].rearrange("p (h s x) -> p h s x", h=4, s=2)[:, :, 0, :],
                    qf[:, sl].rearrange("p (h s x) -> p h s x", h=4, s=2)[:, :, 1, :],
                    tbs[:].rearrange("p (h s x) -> p h s x", h=4, s=2)[:, :, 0, :],
                    ALU.mult), inc="sVE")
                P.op("vector", lambda e, sl=sl: e.tensor_tensor(
                    qm2[:].rearrange("p (h s x) -> p h s x", h=4, s=2)[:, :, 1, :],
                    qf[:, sl].rearrange("p (h s x) -> p h s x", h=4, s=2)[:, :, 0, :],
                    tbs[:].rearrange("p (h s x) -> p h s x", h=4, s=2)[:, :, 1, :],
                    ALU.mult), inc="sVE")
                P.op("vector", lambda e, sl=sl: e.tensor_tensor(
                    qrot[:, sl], qm1[:, sl], qm2[:], ALU.add),
                    waits=W(qrot_free[0]) if si == 0 else (), inc="sVE")
                last_ms = P.ms("sVE")
                tb_free = last_ms
            qrope_done = last_ms
            qf_free = qrope_done
            qrot_free[0] = P.dma("sync", lambda e, cc=cc: e.dma_start(
                out=qr_sp[cc * 128:(cc + 1) * 128, :], in_=qrot[:]),
                "dQr", waits=[qrope_done])
        fold_step()
        fold_step()
    while fold_i[0] < len(fold_jobs):
        fold_step()

    P.barrier()

    al = mkalloc("p2")
    ktb = [al(f"ktb{i}", (128, S), BF16) for i in range(2)]
    vtb = [al(f"vtb{i}", (128, NCH * 128), BF16) for i in range(2)]
    qtb = [al(f"qtb{i}", (128, TME), BF16) for i in range(4)]
    expS = [al(f"expS{i}", (128, 2048), BF16) for i in range(2)]
    sumacc = al("sumacc", (128, 2048), F32)
    sumacc2 = al("sumacc2", (128, 2048), F32)
    av_f = [al(f"avf{i}", (128, 512), F32) for i in range(2)]
    rep_f = [al(f"repf{i}", (128, 512), F32) for i in range(2)]
    att_t = [al(f"att{i}", (128, 512), BF16) for i in range(4)]
    d_row = [al(f"drow{i}", (1, 512), F32) for i in range(2)]

    kv_free = [None, None]
    kv_ready = [None, None]
    qt_free = [None] * 4
    qt_ready = [None] * 4
    expS_free = [None, None]
    att_free = [None] * 4
    avf_free = [None, None]
    repf_free = [None, None]
    drow_free = [None, None]

    def prefetch_kv(g):
        b = g % 2
        P.dma("sync", lambda e, g=g, b=b: e.dma_start(
            out=ktb[b][:], in_=kr_sp[:, g * 128:(g + 1) * 128], transpose=True),
            f"dKv{b}", waits=W(kv_free[b]))
        kv_ready[b] = P.dma("sync", lambda e, g=g, b=b: e.dma_start(
            out=vtb[b][:].rearrange("p (c x) -> p c x", x=128),
            in_=v_sp[:, :, g * 128:(g + 1) * 128].transpose([1, 0, 2])), f"dKv{b}")

    def prefetch_q(h, slot):
        qt_ready[slot] = P.dma("sync", lambda e, h=h, slot=slot: e.dma_start(
            out=qtb[slot][:], in_=qr_sp[:, h * 128:(h + 1) * 128], transpose=True),
            f"dQs{slot}", waits=W(qt_free[slot]))

    prefetch_kv(0)
    prefetch_q(0, 0)
    prefetch_q(1, 1)
    pA_free2 = None
    exph_ms = [None, None]
    pDE_free = [None]
    sumacc_free = [None]
    pBC_free2 = [None, None]
    pend_epi = []

    def mk_epi(sum_done_v, sum_done_p, avf0, avf1, h0, h1, qt):
        box = {'avf': [avf0, avf1]}

        def s1():
            for hi in range(2):
                for k2 in range(2):
                    P.op("tensor", lambda e, hi=hi, k2=k2: e.matmul(
                        (pD if hi == 0 else pE)[0:1, :], ones_cf[:],
                        sumacc[:, hi * 1024 + k2 * 512: hi * 1024 + (k2 + 1) * 512],
                        start=(k2 == 0), stop=(k2 == 1)),
                        waits=([sum_done_v] + W(pDE_free[0]))
                              if (hi == 0 and k2 == 0) else (),
                        inc="sPE" if (hi == 1 and k2 == 1) else None)
            box['sums'] = P.ms("sPE")
            sumacc_free[0] = box['sums']

        def s2():
            P.op("vector", lambda e: e.tensor_copy(d_row[0][:], pD[0:1, :]),
                 waits=[box['sums']] + W(drow_free[0], drow_free[1]))
            P.op("vector", lambda e: e.tensor_copy(d_row[1][:], pE[0:1, :]))
            P.op("vector", lambda e: e.reciprocal(d_row[0][:], d_row[0][:]))
            P.op("vector", lambda e: e.reciprocal(d_row[1][:], d_row[1][:]))
            box['rcp'] = P.ms("sVE")

        def s3():
            P.op("tensor", lambda e: e.matmul(
                pD[:], onesr[:], d_row[0][:], start=True, stop=True),
                waits=W(box['rcp']))
            P.op("tensor", lambda e: e.matmul(
                pE[:], onesr[:], d_row[1][:], start=True, stop=True),
                inc="sPE")
            box['reps'] = P.ms("sPE")
            drow_free[0] = box['reps']
            drow_free[1] = box['reps']

        def s4():
            for hi in range(2):
                P.op("scalar", lambda e, hi=hi: e.copy(
                    rep_f[hi][:], (pD if hi == 0 else pE)[:]),
                    waits=([box['reps']] if hi == 0 else []) + W(repf_free[hi]),
                    inc="sSE")
            box['evs'] = P.ms("sSE")
            pDE_free[0] = box['evs']

        def s5():
            for hi, hh in enumerate((h0, h1)):
                slot = (qt * 2 + hi) % 4
                P.op("vector", lambda e, hi=hi, slot=slot: e.tensor_tensor(
                    att_t[slot][:], av_f[hi][:], rep_f[hi][:], ALU.mult),
                    waits=[box['evs'], box['avf'][hi]] + W(att_free[slot]))
                nrm = P.ms("sVE")
                avf_free[hi] = nrm
                repf_free[hi] = nrm
                att_free[slot] = P.dma("sync",
                    lambda e, hh=hh, qt=qt, slot=slot: e.dma_start(
                        out=at_sp[hh][:, qt * 512:(qt + 1) * 512],
                        in_=att_t[slot][:]),
                    f"dAt{slot}", waits=[nrm])

        return [(1, s1), (3, s2), (5, s3), (7, s4), (9, s5)]

    for g in range(NKV):
        gb = g % 2
        if g + 1 < NKV:
            prefetch_kv(g + 1)
        for pr in range(2):
            pi = g * 2 + pr
            h0, h1 = 4 * g + 2 * pr, 4 * g + 2 * pr + 1
            s0, s1 = (2 * pi) % 4, (2 * pi + 1) % 4
            if pi + 1 < 8:
                prefetch_q(h0 + 2 if pr == 0 else 4 * (g + 1), (s0 + 2) % 4)
                prefetch_q((h0 + 3) if pr == 0 else 4 * (g + 1) + 1, (s1 + 2) % 4)
            for qt in range(NQT):
                av_ms = [None, None]
                for kk in range(32):
                    kc = kk * 2
                    eb = expS[(kk - 1) % 2]
                    kcp = (kk - 1) * 2
                    first = (kk - 1 == 0)
                    while pend_epi and kk >= pend_epi[0][0]:
                        pend_epi.pop(0)[1]()
                    if kk >= 1:
                        seng = "vector"
                        sacc = sumacc
                        if kk - 1 == 0:
                            P.op(seng, lambda e, eb=eb, sacc=sacc: e.tensor_copy(
                                sacc[:], eb[:]),
                                waits=W(exph_ms[0], exph_ms[1], sumacc_free[0]))
                        else:
                            P.op(seng, lambda e, eb=eb, sacc=sacc: e.tensor_tensor(
                                sacc[:], sacc[:], eb[:], ALU.add),
                                waits=W(exph_ms[0], exph_ms[1]))
                        sum_ms_v = P.ms("sVE")
                    for hi, ss in ((0, s0), (1, s1)):
                        w0 = W(exph_ms[hi])
                        if kk == 0:
                            w0 += W(kv_ready[gb], qt_ready[ss], pA_free2)
                        for k2 in range(2):
                            P.op("tensor", lambda e, gb=gb, kc=kc, k2=k2, ss=ss, hi=hi, qt=qt: e.matmul(
                                pA[:, hi * 1024 + k2 * 512: hi * 1024 + (k2 + 1) * 512],
                                ktb[gb][:, (kc + k2) * 128:(kc + k2 + 1) * 128],
                                qtb[ss][:, qt * 512:(qt + 1) * 512],
                                start=True, stop=True),
                                waits=w0 if k2 == 0 else (),
                                inc="sPE" if k2 == 1 else None)
                        sh_ms = P.ms("sPE")
                        if kk >= 1:
                            for k2 in range(2):
                                P.op("tensor", lambda e, gb=gb, kcp=kcp, k2=k2, hi=hi, eb=eb, first=first: e.matmul(
                                    (pB if hi == 0 else pC)[:],
                                    vtb[gb][:, (kcp + k2) * 128:(kcp + k2 + 1) * 128],
                                    eb[:, hi * 1024 + k2 * 512: hi * 1024 + (k2 + 1) * 512],
                                    start=(first and k2 == 0), stop=False),
                                    waits=W(pBC_free2[hi]) if (first and k2 == 0) else (),
                                    inc="sPE" if k2 == 1 else None)
                            av_ms[hi] = P.ms("sPE")
                        we = [sh_ms]
                        if hi == 0 and expS_free[kk % 2]:
                            we += expS_free[kk % 2]
                        P.op("scalar", lambda e, kk=kk, hi=hi: e.activation(
                            expS[kk % 2][:, hi * 1024:(hi + 1) * 1024],
                            pA[:, hi * 1024:(hi + 1) * 1024], AF.Exp, bias=cexpb[:]),
                            waits=we, inc="sSE")
                        exph_ms[hi] = P.ms("sSE")
                    if kk >= 1:
                        expS_free[(kk - 1) % 2] = [av_ms[0], av_ms[1], sum_ms_v]
                eb = expS[31 % 2]
                kcp = 62
                for hi in range(2):
                    for k2 in range(2):
                        P.op("tensor", lambda e, gb=gb, kcp=kcp, k2=k2, hi=hi, eb=eb: e.matmul(
                            (pB if hi == 0 else pC)[:],
                            vtb[gb][:, (kcp + k2) * 128:(kcp + k2 + 1) * 128],
                            eb[:, hi * 1024 + k2 * 512: hi * 1024 + (k2 + 1) * 512],
                            start=False, stop=(k2 == 1)),
                            waits=[exph_ms[hi]] if k2 == 0 else (),
                            inc="sPE" if k2 == 1 else None)
                    av_ms[hi] = P.ms("sPE")
                P.op("vector", lambda e, eb=eb: e.tensor_tensor(
                    sumacc[:], sumacc[:], eb[:], ALU.add),
                    waits=W(exph_ms[0], exph_ms[1]))
                sum_done_v = P.ms("sVE")
                sum_done_p = P.ms("sPO")
                expS_free[1] = [av_ms[0], av_ms[1], sum_done_v]
                avf_in = [None, None]
                for hi in range(2):
                    P.op("scalar", lambda e, hi=hi: e.copy(
                        av_f[hi][:], pB[:] if hi == 0 else pC[:]),
                        waits=[av_ms[hi]] + W(avf_free[hi]), inc="sSE")
                    avf_in[hi] = P.ms("sSE")
                    pBC_free2[hi] = avf_in[hi]
                pend_epi.extend(mk_epi(sum_done_v, sum_done_p,
                                       avf_in[0], avf_in[1], h0, h1, qt))
            pairdone = P.ms("sPE")
            qt_free[s0] = pairdone
            qt_free[s1] = pairdone
        kv_free[gb] = P.ms("sPE")
    while pend_epi:
        pend_epi.pop(0)[1]()

    P.barrier()

    al = mkalloc("p3", limit=H_OFF)
    wo_sb = al("wo", (128, NJ * 2048), BF16)
    abuf = al("abuf", (128, NH * 512), BF16)
    x1pre = al("x1pre", (128, NJ * 512), BF16)
    xres = [al(f"xres{i}", (128, 2048), F32) for i in range(2)]
    wo_f = [al(f"wof{i}", (128, 512), F32) for i in range(2)]
    sq_t = al("sq", (128, 512), BF16)
    mrep_f = al("mrep", (128, 512), F32)
    rxrep_f = al("rxrep", (128, 512), F32)
    rows = al("rows", (1, 6 * 512), F32)
    r_m = rows[0:1, 0:512]; r_s = rows[0:1, 512:1024]; r_t = rows[0:1, 1024:1536]
    r_u = rows[0:1, 1536:2048]; r_q = rows[0:1, 2048:2560]; r_r = rows[0:1, 2560:3072]

    wo_ready = P.dma("gpsimd", lambda e: e.dma_start(
        out=wo_sb[:].rearrange("p (i n) -> p i n", n=2048),
        in_=wo.rearrange("(i p) n -> p i n", p=128)), "dWo")

    abuf_free = None
    xres_free = [None, None]
    pB_free = None
    pA_free3 = None
    wof_free = [None, None]
    mrep_free = None
    sq_free = None
    x1pre_free = None

    for qt in range(NQT):
        a_ready = P.dma("sync", lambda e, qt=qt: e.dma_start(
            out=abuf[:].rearrange("p (i x) -> p i x", x=512),
            in_=at_sp[:, :, qt * 512:(qt + 1) * 512].transpose([1, 0, 2])),
            "dAbuf", waits=W(abuf_free))
        x1_init = []
        for tc in range(4):
            tb = tc % 2
            xr = P.dma("sync", lambda e, qt=qt, tc=tc, tb=tb: e.dma_start(
                out=xres[tb][:],
                in_=xmine[(qt * 4 + tc) * 128:(qt * 4 + tc + 1) * 128, :]),
                f"dXr{tb}", waits=W(xres_free[tb]))
            for j in range(NJ):
                P.op("tensor", lambda e, j=j, tb=tb: e.transpose(
                    pA[:, j * 128:(j + 1) * 128], xres[tb][:, j * 128:(j + 1) * 128],
                    sid_f[:]),
                    waits=([xr] + W(pA_free3)) if j == 0 else (),
                    inc="sPE" if j == NJ - 1 else None)
            tp = P.ms("sPE")
            xres_free[tb] = tp
            P.op("scalar", lambda e, tc=tc: e.copy(
                x1pre[:].rearrange("p (j x) -> p j x", x=512)[:, :, tc * 128:(tc + 1) * 128],
                pA[:].rearrange("p (j x) -> p j x", x=128)),
                waits=[tp] + (W(x1pre_free) if tc == 0 else []), inc="sSE")
            pA_free3 = P.ms("sSE")
            x1_init.append(pA_free3)
        stats_start = None
        for j in range(NJ):
            jb = j % 2
            for i in range(NJ):
                P.op("tensor", lambda e, i=i, j=j: e.matmul(
                    pB[:], wo_sb[:, i * 2048 + j * 128: i * 2048 + (j + 1) * 128],
                    abuf[:, i * 512:(i + 1) * 512],
                    start=(i == 0), stop=(i == NJ - 1)),
                    waits=([a_ready, wo_ready] + W(pB_free)) if i == 0 else (),
                    inc="sPE" if i == NJ - 1 else None)
            mm = P.ms("sPE")
            P.op("scalar", lambda e, jb=jb: e.copy(wo_f[jb][:], pB[:]),
                 waits=[mm] + W(wof_free[jb]), inc="sSE")
            pB_free = P.ms("sSE")
            P.op("vector", lambda e, j=j, jb=jb: e.tensor_tensor(
                x1pre[:, j * 512:(j + 1) * 512], x1pre[:, j * 512:(j + 1) * 512],
                wo_f[jb][:], ALU.add),
                waits=[pB_free] + (x1_init if j == 0 else []), inc="sVE")
            x1_ms = P.ms("sVE")
            wof_free[jb] = x1_ms
            P.op("vector", lambda e, j=j: e.tensor_tensor(
                sq_t[:], x1pre[:, j * 512:(j + 1) * 512],
                x1pre[:, j * 512:(j + 1) * 512], ALU.mult),
                waits=W(sq_free), inc="sVE")
            sq_ms = P.ms("sVE")
            P.op("tensor", lambda e, j=j: e.matmul(
                pD[0:1, :], mean_c[:], x1pre[:, j * 512:(j + 1) * 512],
                start=(j == 0), stop=(j == NJ - 1)),
                waits=[x1_ms] + (W(mrep_free) if j == 0 else []))
            P.op("tensor", lambda e, j=j: e.matmul(
                pE[0:1, :], mean_c[:], sq_t[:],
                start=(j == 0), stop=(j == NJ - 1)), waits=[sq_ms],
                inc="sPE")
            sq_free = P.ms("sPE")
        stats_done = P.ms("sPE")
        abuf_free = stats_done
        P.op("scalar", lambda e: e.copy(r_m, pD[0:1, :]), waits=[stats_done], inc="sSE")
        P.op("scalar", lambda e: e.copy(r_s, pE[0:1, :]), inc="sSE")
        rows_ms = P.ms("sSE")
        P.op("vector", lambda e: e.tensor_tensor(r_t, r_m, r_m, ALU.mult),
             waits=[rows_ms], inc="sVE")
        P.op("vector", lambda e: e.tensor_tensor(r_s, r_s, r_t, ALU.subtract), inc="sVE")
        P.op("vector", lambda e: e.tensor_scalar(r_u, r_s, 1e-5, None, ALU.add), inc="sVE")
        P.op("vector", lambda e: e.reciprocal(r_t, r_u), inc="sVE")
        P.op("vector", lambda e: e.tensor_tensor(r_q, r_s, r_t, ALU.mult), inc="sVE")
        P.op("vector", lambda e: e.tensor_scalar(r_q, r_q, 1e-6, None, ALU.add), inc="sVE")
        vr = P.ms("sVE")
        P.op("scalar", lambda e: e.activation(r_u, r_u, AF.Ln), waits=[vr], inc="sSE")
        P.op("scalar", lambda e: e.activation(r_r, r_u, AF.Exp, scale=chn[0:1, 0:1]), inc="sSE")
        P.op("scalar", lambda e: e.activation(r_q, r_q, AF.Ln), inc="sSE")
        P.op("scalar", lambda e: e.activation(r_t, r_q, AF.Exp, scale=chn[0:1, 0:1]), inc="sSE")
        P.op("scalar", lambda e, qt=qt: e.activation(
            rsinv_row[0:1, qt * 512:(qt + 1) * 512], r_q, AF.Exp, scale=chp[0:1, 0:1]),
            inc="sSE")
        srow = P.ms("sSE")
        P.op("vector", lambda e: e.tensor_tensor(r_r, r_r, r_t, ALU.mult),
             waits=[srow], inc="sVE")
        rxm = P.ms("sVE")
        P.op("tensor", lambda e: e.matmul(pD[:], onesr[:], r_m, start=True, stop=True),
             waits=[rxm])
        P.op("tensor", lambda e: e.matmul(pE[:], onesr[:], r_r, start=True, stop=True),
             inc="sPE")
        repd = P.ms("sPE")
        P.op("scalar", lambda e: e.copy(mrep_f[:], pD[:]),
             waits=[repd] + W(x1pre_free), inc="sSE")
        P.op("scalar", lambda e: e.copy(rxrep_f[:], pE[:]), inc="sSE")
        repev = P.ms("sSE")
        mrep_free = repev
        for j in range(NJ):
            jb = j % 2
            P.op("vector", lambda e, j=j, jb=jb: e.tensor_tensor(
                wo_f[jb][:], x1pre[:, j * 512:(j + 1) * 512], mrep_f[:], ALU.subtract),
                waits=[repev] + W(wof_free[jb]), inc="sVE")
            P.op("vector", lambda e, j=j, jb=jb, qt=qt: e.tensor_tensor(
                h_sb[:, j * 2048 + qt * 512: j * 2048 + (qt + 1) * 512],
                wo_f[jb][:], rxrep_f[:], ALU.mult), inc="sVE")
            wof_free[jb] = P.ms("sVE")
        x1pre_free = P.ms("sVE")
    h_done = P.ms("sVE")

    P.barrier()

    al = mkalloc("p4", limit=MOE_OFF)
    act = al("act", (128, NFC * 2048), BF16)
    wgu = [al(f"wgu{i}", (128, NJ * 128), BF16) for i in range(3)]
    wdt = [al(f"wdt{i}", (128, NFC * 128), BF16) for i in range(2)]
    wrep = al("wrep", (128, 2048), BF16)
    u_t = al("ut", (128, 2048), BF16)
    d_f = [al(f"df{i}", (128, 512), F32) for i in range(1)]
    rtw_sb = al("rtw", (128, NJ * 8), BF16)
    lg_t = al("lg", (8, 512), F32)
    lgn = al("lgn", (128, 32), F32)
    lge = al("lge", (128, 32), F32)
    weq = al("weq", (128, 32), F32)
    wnat = al("wnat", (128, 32), F32)
    wmax = al("wmax", (128, 4), F32)
    wsum = al("wsum", (128, 4), F32)
    wv1 = al("wv1", (128, 4), F32)
    wv2 = al("wv2", (128, 4), F32)
    wden = al("wden", (128, 4), F32)
    wT_s = al("wTs", (8, TME), BF16)
    wrow = al("wrow", (1, 512), BF16)

    rt_ld = P.dma("gpsimd", lambda e: e.dma_start(
        out=rtw_sb[:].rearrange("p (j n) -> p j n", n=8),
        in_=rtw.rearrange("(j p) n -> p j n", p=128),
        allow_slow_non_contiguous=True), "dRtw")
    for j in range(NJ):
        P.op("vector", lambda e, j=j: e.tensor_scalar(
            rtw_sb[:, j * 8:(j + 1) * 8], rtw_sb[:, j * 8:(j + 1) * 8],
            rw_sb[:, j:j + 1], None, ALU.mult),
            waits=[rt_ld] if j == 0 else (), inc="sVE")
    rt_folded = P.ms("sVE")
    wT_done = []
    lg_free = None
    for qt in range(NQT):
        for j in range(NJ):
            P.op("tensor", lambda e, j=j, qt=qt: e.matmul(
                pD[0:8, :], rtw_sb[:, j * 8:(j + 1) * 8],
                h_sb[:, j * 2048 + qt * 512: j * 2048 + (qt + 1) * 512],
                start=(j == 0), stop=(j == NJ - 1)),
                waits=([rt_folded] + W(lg_free)) if j == 0 else (),
                inc="sPE" if j == NJ - 1 else None)
        lg_done = P.ms("sPE")
        P.op("scalar", lambda e: e.copy(lg_t[:], pD[0:8, :]),
             waits=[lg_done] + W(lg_free), inc="sSE")
        lg_ev = P.ms("sSE")
        for tc in range(4):
            P.op("tensor", lambda e, tc=tc: e.transpose(
                pE[:, tc * 8:(tc + 1) * 8], lg_t[0:8, tc * 128:(tc + 1) * 128], sid_f[0:8, 0:8]),
                waits=[lg_ev] if tc == 0 else (), inc="sPE" if tc == 3 else None)
        tpd = P.ms("sPE")
        P.op("scalar", lambda e: e.copy(lgn[:], pE[:, 0:32]), waits=[tpd], inc="sSE")
        lgn_ev = P.ms("sSE")
        P.op("vector", lambda e: e.tensor_reduce(
            wmax[:], lgn[:].rearrange("p (t x) -> p t x", x=8), AX.X, ALU.max),
            waits=[lgn_ev], inc="sVE")
        for tc in range(4):
            P.op("vector", lambda e, tc=tc: e.tensor_scalar(
                lgn[:, tc * 8:(tc + 1) * 8], lgn[:, tc * 8:(tc + 1) * 8],
                wmax[:, tc:tc + 1], None, ALU.subtract), inc="sVE")
        sub_ms = P.ms("sVE")
        P.op("scalar", lambda e: e.activation(lge[:], lgn[:], AF.Exp),
             waits=[sub_ms], inc="sSE")
        exp_ms = P.ms("sSE")
        P.op("vector", lambda e: e.tensor_reduce(
            wsum[:], lge[:].rearrange("p (t x) -> p t x", x=8), AX.X, ALU.add),
            waits=[exp_ms], inc="sVE")
        P.op("vector", lambda e: e.reciprocal(wsum[:], wsum[:]), inc="sVE")
        for tc in range(4):
            P.op("vector", lambda e, tc=tc: e.tensor_scalar(
                lge[:, tc * 8:(tc + 1) * 8], lge[:, tc * 8:(tc + 1) * 8],
                wsum[:, tc:tc + 1], None, ALU.mult), inc="sVE")
        P.op("vector", lambda e: e.tensor_reduce(
            wv1[:], lge[:].rearrange("p (t x) -> p t x", x=8), AX.X, ALU.max), inc="sVE")
        for tc in range(4):
            P.op("vector", lambda e, tc=tc: e.tensor_scalar(
                weq[:, tc * 8:(tc + 1) * 8], lge[:, tc * 8:(tc + 1) * 8],
                wv1[:, tc:tc + 1], None, ALU.is_equal), inc="sVE")
        P.op("vector", lambda e: e.tensor_scalar(weq[:], weq[:], 1e9, None, ALU.mult),
             inc="sVE")
        P.op("vector", lambda e: e.tensor_tensor(weq[:], lge[:], weq[:], ALU.subtract),
             inc="sVE")
        P.op("vector", lambda e: e.tensor_reduce(
            wv2[:], weq[:].rearrange("p (t x) -> p t x", x=8), AX.X, ALU.max), inc="sVE")
        P.op("vector", lambda e: e.tensor_tensor(wden[:], wv1[:], wv2[:], ALU.add), inc="sVE")
        P.op("vector", lambda e: e.reciprocal(wden[:], wden[:]), inc="sVE")
        for tc in range(4):
            P.op("vector", lambda e, tc=tc: e.tensor_scalar(
                wnat[:, tc * 8:(tc + 1) * 8], lge[:, tc * 8:(tc + 1) * 8],
                wv2[:, tc:tc + 1], None, ALU.is_ge), inc="sVE")
        P.op("vector", lambda e: e.tensor_tensor(wnat[:], wnat[:], lge[:], ALU.mult),
             inc="sVE")
        for tc in range(4):
            P.op("vector", lambda e, tc=tc: e.tensor_scalar(
                wnat[:, tc * 8:(tc + 1) * 8], wnat[:, tc * 8:(tc + 1) * 8],
                wden[:, tc:tc + 1], None, ALU.mult), inc="sVE")
        w_ms = P.ms("sVE")
        for tc in range(4):
            P.op("tensor", lambda e, tc=tc: e.transpose(
                pD[0:8, tc * 128:(tc + 1) * 128], wnat[:, tc * 8:(tc + 1) * 8], sid_f[:]),
                waits=[w_ms, lg_ev] if tc == 0 else (), inc="sPE" if tc == 3 else None)
        wtp = P.ms("sPE")
        P.op("scalar", lambda e, qt=qt: e.copy(
            wT_s[0:8, qt * 512:(qt + 1) * 512], pD[0:8, :]), waits=[wtp], inc="sSE")
        lg_free = P.ms("sSE")
        wT_done.append(lg_free)

    UPB = (pB, pC, pD, pE)
    ucp_free = None
    gu_ld = [None] * 4
    gu_free = [None] * 4
    wdt_free = [None, None]
    wdt_ld = [None, None]
    act_free_fc = [None] * NFC
    ut_free = None
    df_free = [None]
    moe_w = []
    wrep_free = None
    wrow_free = None
    prev_act_ms = None
    pA_free4 = None
    pBC_free = [None, None]

    for eidx in range(9):
        shared = (eidx == 0)
        ex = eidx - 1
        mg = 16 if shared else ex
        mu = 17 if shared else 8 + ex
        wdsrc = swd if shared else wd[ex]
        if not shared:
            for qt in range(NQT):
                wr_ld = P.dma("sync", lambda e, ex=ex, qt=qt: e.dma_start(
                    out=wrow[:], in_=wT_s[ex:ex + 1, qt * 512:(qt + 1) * 512]),
                    "dWrow",
                    waits=(wT_done + W(wrow_free)) if qt == 0 else W(wrow_free))
                P.op("tensor", lambda e: e.matmul(
                    pE[:], onesr_b[:], wrow[:], start=True, stop=True),
                    waits=[wr_ld] + W(wrep_free, ucp_free), inc="sPE")
                wr_mm = P.ms("sPE")
                wrow_free = wr_mm
                P.op("scalar", lambda e, qt=qt: e.copy(
                    wrep[:, qt * 512:(qt + 1) * 512], pE[:]),
                    waits=[wr_mm] + (W(prev_act_ms) if qt == 0 else []), inc="sSE")
                wrep_free = P.ms("sSE")
            wrep_ready = P.ms("sSE")
        else:
            wrep_ready = None
        for fc in range(NFC):
            sg = fc % 2
            su = 2
            g_ready = P.dma("gpsimd", lambda e, mg=mg, fc=fc, sg=sg: e.dma_start(
                out=wgu[sg][:].rearrange("p (j f) -> p j f", f=128),
                in_=wf_sp[mg, :, :, fc * 128:(fc + 1) * 128].transpose([1, 0, 2])),
                f"dGu{sg}", waits=W(gu_free[sg]))
            u_ready = P.dma("gpsimd", lambda e, mu=mu, fc=fc, su=su: e.dma_start(
                out=wgu[su][:].rearrange("p (j f) -> p j f", f=128),
                in_=wf_sp[mu, :, :, fc * 128:(fc + 1) * 128].transpose([1, 0, 2])),
                f"dGu{su}", waits=W(gu_free[su]))
            gu_ready = [g_ready, u_ready]
            for j in range(NJ):
                for qt in range(NQT):
                    P.op("tensor", lambda e, j=j, qt=qt, sg=sg: e.matmul(
                        pA[:, qt * 512:(qt + 1) * 512],
                        wgu[sg][:, j * 128:(j + 1) * 128],
                        h_sb[:, j * 2048 + qt * 512: j * 2048 + (qt + 1) * 512],
                        start=(j == 0), stop=(j == NJ - 1)),
                        waits=([g_ready] + W(pA_free4)) if (j == 0 and qt == 0) else (),
                        inc="sPE" if (j == NJ - 1 and qt == NQT - 1) else None)
            g_done = P.ms("sPE")
            P.op("scalar", lambda e, fc=fc: e.activation(
                act[:, fc * 2048:(fc + 1) * 2048], pA[:], AF.Silu),
                waits=[g_done] + W(act_free_fc[fc]), inc="sSE")
            pA_free4 = P.ms("sSE")
            up_w0 = [u_ready] + W(ucp_free)
            if fc == 0:
                up_w0 += wT_done if shared else [wrep_ready]
                up_w0 += W(pBC_free[0], pBC_free[1])
            for j in range(NJ):
                for qt in range(NQT):
                    P.op("tensor", lambda e, j=j, qt=qt, su=su: e.matmul(
                        UPB[qt][:],
                        wgu[su][:, j * 128:(j + 1) * 128],
                        h_sb[:, j * 2048 + qt * 512: j * 2048 + (qt + 1) * 512],
                        start=(j == 0), stop=(j == NJ - 1)),
                        waits=up_w0 if (j == 0 and qt == 0) else (),
                        inc="sPE" if (j == NJ - 1 and qt == NQT - 1) else None)
            u_done = P.ms("sPE")
            gu_free[sg] = u_done
            gu_free[su] = u_done
            for qt in range(NQT):
                P.op("scalar", lambda e, qt=qt: e.copy(
                    u_t[:, qt * 512:(qt + 1) * 512], UPB[qt][:]),
                    waits=([u_done] + W(ut_free)) if qt == 0 else (), inc="sSE")
            ucp_free = P.ms("sSE")
            P.op("vector", lambda e, fc=fc: e.tensor_tensor(
                act[:, fc * 2048:(fc + 1) * 2048], act[:, fc * 2048:(fc + 1) * 2048],
                u_t[:], ALU.mult), waits=[ucp_free], inc="sVE")
            if not shared:
                P.op("vector", lambda e, fc=fc: e.tensor_tensor(
                    act[:, fc * 2048:(fc + 1) * 2048], act[:, fc * 2048:(fc + 1) * 2048],
                    wrep[:], ALU.mult), waits=[wrep_ready], inc="sVE")
            act_ms = P.ms("sVE")
            ut_free = act_ms
            act_free_fc[fc] = None
        act_ready = P.ms("sVE")
        for od in range(NJ):
            ob = od % 2
            wdt_ld[ob] = P.dma("gpsimd", lambda e, od=od, ob=ob, wdsrc=wdsrc: e.dma_start(
                out=wdt[ob][:].rearrange("p (fc o) -> p fc o", o=128),
                in_=wdsrc[:, od * 128:(od + 1) * 128].rearrange("(fc p) o -> p fc o", p=128)),
                f"dWdt{ob}", waits=W(wdt_free[ob]))
            for qt in range(NQT):
                pp = pB if (od * NQT + qt) % 2 == 0 else pC
                ppi = (od * NQT + qt) % 2
                for fc in range(NFC):
                    P.op("tensor", lambda e, fc=fc, qt=qt, ob=ob, pp=pp: e.matmul(
                        pp[:], wdt[ob][:, fc * 128:(fc + 1) * 128],
                        act[:, fc * 2048 + qt * 512: fc * 2048 + (qt + 1) * 512],
                        start=(fc == 0), stop=(fc == NFC - 1)),
                        waits=([wdt_ld[ob], act_ready] + W(pBC_free[ppi]))
                              if fc == 0 else (),
                        inc="sPE" if fc == NFC - 1 else None)
                dmm = P.ms("sPE")
                if qt == NQT - 1:
                    wdt_free[ob] = dmm
                msl = moe[:, od * 2048 + qt * 512: od * 2048 + (qt + 1) * 512]
                if shared:
                    wmoe = [dmm]
                    if od == 0 and qt == 0:
                        wmoe = wmoe + wT_done
                    P.op("scalar", lambda e, msl=msl, pp=pp: e.copy(msl, pp[:]),
                         waits=wmoe, inc="sSE")
                    pBC_free[ppi] = P.ms("sSE")
                    moe_w.append(pBC_free[ppi])
                else:
                    P.op("scalar", lambda e, pp=pp: e.copy(d_f[0][:], pp[:]),
                         waits=[dmm] + W(df_free[0]), inc="sSE")
                    pBC_free[ppi] = P.ms("sSE")
                    dev = pBC_free[ppi]
                    P.op("vector", lambda e, msl=msl: e.tensor_tensor(
                        msl, msl, d_f[0][:], ALU.add), waits=[dev], inc="sVE")
                    df_free[0] = P.ms("sVE")
                    moe_w.append(df_free[0])
        alldown = P.ms("sPE")
        for fc in range(NFC):
            act_free_fc[fc] = alldown
        prev_act_ms = P.ms("sVE")

    P.barrier()

    al = mkalloc("p5", limit=MOE_OFF)
    x2pre = al("x2pre", (128, NJ * 512), BF16)
    x2f = al("x2f", (128, NJ * 512), BF16)
    xout = al("xout", (128, 2048), F32)
    t_f = [al(f"tf{i}", (128, 512), F32) for i in range(2)]
    rsrep_f = al("rsrep", (128, 512), F32)
    m2rep_f = al("m2rep", (128, 512), F32)
    r2rep_f = al("r2rep", (128, 512), F32)
    rows5 = al("rows5", (1, 4 * 512), F32)
    q_m = rows5[0:1, 0:512]; q_s = rows5[0:1, 512:1024]
    q_t = rows5[0:1, 1024:1536]; q_u = rows5[0:1, 1536:2048]
    sq5 = al("sq5", (128, 512), BF16)

    tf_free = [None, None]
    xout_free = None
    pA_free5 = None
    rep_free5 = None
    x2pre_prev = None
    sq5_free = None
    out_dmas = []

    for qt in range(NQT):
        P.op("tensor", lambda e, qt=qt: e.matmul(
            pD[:], onesr[:], rsinv_row[0:1, qt * 512:(qt + 1) * 512],
            start=True, stop=True), waits=W(rep_free5), inc="sPE")
        rsd = P.ms("sPE")
        P.op("scalar", lambda e: e.copy(rsrep_f[:], pD[:]),
             waits=[rsd] + W(x2pre_prev), inc="sSE")
        rsr = P.ms("sSE")
        for j in range(NJ):
            jb = j % 2
            P.op("vector", lambda e, j=j, jb=jb, qt=qt: e.tensor_tensor(
                t_f[jb][:], h_sb[:, j * 2048 + qt * 512: j * 2048 + (qt + 1) * 512],
                rsrep_f[:], ALU.mult), waits=[rsr] + W(tf_free[jb]), inc="sVE")
            P.op("vector", lambda e, j=j, jb=jb, qt=qt: e.tensor_tensor(
                x2pre[:, j * 512:(j + 1) * 512], t_f[jb][:],
                moe[:, j * 2048 + qt * 512: j * 2048 + (qt + 1) * 512], ALU.add),
                inc="sVE")
            x2_ms = P.ms("sVE")
            tf_free[jb] = x2_ms
            P.op("vector", lambda e, j=j: e.tensor_tensor(
                sq5[:], x2pre[:, j * 512:(j + 1) * 512],
                x2pre[:, j * 512:(j + 1) * 512], ALU.mult),
                waits=W(sq5_free), inc="sVE")
            sq_ms = P.ms("sVE")
            P.op("tensor", lambda e, j=j: e.matmul(
                pD[0:1, :], mean_c[:], x2pre[:, j * 512:(j + 1) * 512],
                start=(j == 0), stop=(j == NJ - 1)), waits=[x2_ms, rsd])
            P.op("tensor", lambda e, j=j: e.matmul(
                pE[0:1, :], mean_c[:], sq5[:],
                start=(j == 0), stop=(j == NJ - 1)), waits=[sq_ms],
                inc="sPE")
            sq5_free = P.ms("sPE")
        st_done = P.ms("sPE")
        P.op("scalar", lambda e: e.copy(q_m, pD[0:1, :]), waits=[st_done], inc="sSE")
        P.op("scalar", lambda e: e.copy(q_s, pE[0:1, :]), inc="sSE")
        rws = P.ms("sSE")
        P.op("vector", lambda e: e.tensor_tensor(q_t, q_m, q_m, ALU.mult),
             waits=[rws], inc="sVE")
        P.op("vector", lambda e: e.tensor_tensor(q_s, q_s, q_t, ALU.subtract), inc="sVE")
        P.op("vector", lambda e: e.tensor_scalar(q_u, q_s, 1e-5, None, ALU.add), inc="sVE")
        vr5 = P.ms("sVE")
        P.op("scalar", lambda e: e.activation(q_u, q_u, AF.Ln), waits=[vr5], inc="sSE")
        P.op("scalar", lambda e: e.activation(q_t, q_u, AF.Exp, scale=chn[0:1, 0:1]),
             inc="sSE")
        sr5 = P.ms("sSE")
        P.op("tensor", lambda e: e.matmul(pD[:], onesr[:], q_m, start=True, stop=True),
             waits=[sr5])
        P.op("tensor", lambda e: e.matmul(pE[:], onesr[:], q_t, start=True, stop=True),
             inc="sPE")
        rp5 = P.ms("sPE")
        P.op("scalar", lambda e: e.copy(m2rep_f[:], pD[:]),
             waits=[rp5] + W(x2pre_prev), inc="sSE")
        P.op("scalar", lambda e: e.copy(r2rep_f[:], pE[:]), inc="sSE")
        rpe5 = P.ms("sSE")
        rep_free5 = rpe5
        for j in range(NJ):
            jb = j % 2
            P.op("vector", lambda e, j=j, jb=jb: e.tensor_tensor(
                t_f[jb][:], x2pre[:, j * 512:(j + 1) * 512], m2rep_f[:], ALU.subtract),
                waits=[rpe5] + W(tf_free[jb]), inc="sVE")
            P.op("vector", lambda e, j=j, jb=jb: e.tensor_tensor(
                x2f[:, j * 512:(j + 1) * 512], t_f[jb][:], r2rep_f[:], ALU.mult),
                inc="sVE")
            tf_free[jb] = P.ms("sVE")
        x2f_done = P.ms("sVE")
        x2pre_prev = x2f_done
        for tc in range(4):
            for j in range(NJ):
                P.op("tensor", lambda e, j=j, tc=tc: e.transpose(
                    pA[:, j * 64:(j + 1) * 64].bitcast(BF16),
                    x2f[:, j * 512 + tc * 128: j * 512 + (tc + 1) * 128], sid_b[:]),
                    waits=([x2f_done] + W(pA_free5)) if j == 0 else (),
                    inc="sPE" if j == NJ - 1 else None)
            otp = P.ms("sPE")
            P.op("scalar", lambda e: e.copy(xout[:], pA[:, 0:1024].bitcast(BF16)),
                 waits=[otp] + W(xout_free), inc="sSE")
            pA_free5 = P.ms("sSE")
            xout_free = P.dma("sync", lambda e, qt=qt, tc=tc: e.dma_start(
                out=dout[(qt * 4 + tc) * 128:(qt * 4 + tc + 1) * 128, :], in_=xout[:]),
                "dOut", waits=[pA_free5])
            out_dmas.append(xout_free)

    P.barrier()

    with ExitStack() as stack:
        sems = {s: stack.enter_context(nc.semaphore(s)) for s in P.cnt}
        with nc.Block() as block:
            def runner(eng_name):
                def run(eng):
                    lastw = {}
                    for waits, fn, inc, amt in P.ops[eng_name]:
                        for sem, val in waits:
                            if lastw.get(sem, -1) < val:
                                eng.wait_ge(sems[sem], val)
                                lastw[sem] = val
                        if fn is None:
                            continue
                        instr = fn(eng)
                        if inc is not None:
                            instr.then_inc(sems[inc], amt)
                return run
            block.tensor(runner("tensor"))
            block.scalar(runner("scalar"))
            block.vector(runner("vector"))
            block.sync(runner("sync"))
            block.gpsimd(runner("gpsimd"))
    return nc


_NC_CACHE = {}


def kernel(L0_attn_hidden_states, L0_attn_Wq, L0_attn_Wk, L0_attn_Wv, L0_attn_Wo,
           L0_ffn_rms_weight, L0_ffn_router_w, L0_ffn_w_gate, L0_ffn_w_up,
           L0_ffn_w_down, L0_ffn_sw_gate, L0_ffn_sw_up, L0_ffn_sw_down):
    x = np.ascontiguousarray(np.asarray(L0_attn_hidden_states, dtype=np.float32))
    perm = np.concatenate([np.arange(0, HD, 2), np.arange(1, HD, 2)])

    def permute_cols(w, nh):
        w = np.asarray(w, dtype=np.float32)
        return np.ascontiguousarray(w.reshape(D, nh, HD)[:, :, perm].reshape(D, nh * HD))

    wq_p = permute_cols(L0_attn_Wq, NH)
    wk_p = permute_cols(L0_attn_Wk, NKV)
    wv = np.ascontiguousarray(np.asarray(L0_attn_Wv, dtype=np.float32))
    wo = np.ascontiguousarray(np.asarray(L0_attn_Wo, dtype=np.float32))
    rw = np.ascontiguousarray(np.asarray(L0_ffn_rms_weight, dtype=np.float32).reshape(D, 1))
    rtw = np.ascontiguousarray(np.asarray(L0_ffn_router_w, dtype=np.float32))
    wg = np.ascontiguousarray(np.asarray(L0_ffn_w_gate, dtype=np.float32))
    wu = np.ascontiguousarray(np.asarray(L0_ffn_w_up, dtype=np.float32))
    wd = np.ascontiguousarray(np.asarray(L0_ffn_w_down, dtype=np.float32))
    swg = np.ascontiguousarray(np.asarray(L0_ffn_sw_gate, dtype=np.float32))
    swu = np.ascontiguousarray(np.asarray(L0_ffn_sw_up, dtype=np.float32))
    swd = np.ascontiguousarray(np.asarray(L0_ffn_sw_down, dtype=np.float32))

    inv = 1.0 / THETA ** (np.arange(0, HD, 2, dtype=np.float64) / HD)
    tpos = np.arange(S, dtype=np.float64)
    fr = np.outer(tpos, inv)
    Cb, Sb = np.cos(fr), np.sin(fr)
    cc1 = np.concatenate([Cb, Cb], axis=1)
    ss1 = np.concatenate([-Sb, Sb], axis=1)
    rkc = np.ascontiguousarray(np.tile(cc1, (1, NKV)).astype(np.float32))
    rks = np.ascontiguousarray(np.tile(ss1, (1, NKV)).astype(np.float32))
    scale = HD ** -0.5
    rqc_full = np.tile(cc1 * scale, (1, NH)).astype(np.float32)
    rqs_full = np.tile(ss1 * scale, (1, NH)).astype(np.float32)
    idb = np.eye(128, dtype=ml_dtypes.bfloat16)
    idf = np.eye(128, dtype=np.float32)

    nc = build_nc()
    in_maps = []
    for c in range(8):
        b = c // 4
        qoff = (c % 4) * TME
        in_maps.append({
            "xfull": x[b], "xmine": np.ascontiguousarray(x[b, qoff:qoff + TME]),
            "wq": wq_p, "wk": wk_p, "wv": wv, "wo": wo, "rw": rw, "rtw": rtw,
            "wg": wg, "wu": wu, "wd": wd, "swg": swg, "swu": swu, "swd": swd,
            "rkc": rkc, "rks": rks,
            "rqc": np.ascontiguousarray(rqc_full[qoff:qoff + TME]),
            "rqs": np.ascontiguousarray(rqs_full[qoff:qoff + TME]),
            "idb": idb, "idf": idf,
        })
    res = run_bass_kernel_spmd(nc, in_maps, core_ids=list(range(8)))
    out = np.empty((2, S, D), dtype=np.float32)
    for c in range(8):
        b = c // 4
        qoff = (c % 4) * TME
        out[b, qoff:qoff + TME] = np.asarray(res.results[c]["out"], dtype=np.float32)
    return out


# version 19
# speedup vs baseline: 1.1050x; 1.0390x over previous
import numpy as np
import ml_dtypes
import concourse.bass as bass
import concourse.mybir as mybir
from concourse.bass_utils import run_bass_kernel_spmd
from contextlib import ExitStack

F32 = mybir.dt.float32
BF16 = mybir.dt.bfloat16
AF = mybir.ActivationFunctionType
ALU = mybir.AluOpType
AX = mybir.AxisListType

S, D, HD, NH, NKV = 8192, 2048, 128, 16, 4
KVD = NKV * HD
TME = 2048
NCH = S // 128
NMY = TME // 128
NJ = D // 128
NQT = TME // 512
E, F, SF = 8, 1024, 1024
NFC = F // 128
SHIFT = 12.0
THETA = 10000.0
SBUF_BASE = 16576
SBUF_TOP = 229312
SBUF_BYTES = SBUF_TOP

ENGS = ("tensor", "scalar", "vector", "sync", "gpsimd")
OWN = {"tensor": "sPE", "scalar": "sSE", "vector": "sVE", "gpsimd": "sPO"}
SELFSYNC = ("scalar", "vector", "gpsimd")


class Plan:
    def __init__(self):
        self.ops = {e: [] for e in ENGS}
        self.cnt = {"sPE": 0, "sSE": 0, "sVE": 0, "sPO": 0}

    def op(self, eng, fn, waits=(), inc=None):
        waits = [w for w in waits if w is not None]
        if eng in SELFSYNC:
            inc = OWN[eng]
            waits.append((inc, self.cnt[inc]))
        if inc is not None:
            self.cnt[inc] += 1
        self.ops[eng].append((tuple(waits), fn, inc, 1))
        return (inc, self.cnt[inc]) if inc else None

    def dma(self, eng, fn, stream, waits=()):
        waits = [w for w in waits if w is not None]
        self.cnt[stream] = self.cnt.get(stream, 0) + 16
        self.ops[eng].append((tuple(waits), fn, stream, 16))
        return (stream, self.cnt[stream])

    def ms(self, sem):
        return (sem, self.cnt.get(sem, 0))

    def barrier(self):
        for eng in ("tensor", "scalar", "vector", "gpsimd"):
            s = OWN[eng]
            self.op(eng, lambda e: e.nop(), waits=[(s, self.cnt[s])], inc=s)
        counts = list(self.cnt.items())
        for eng in ENGS:
            self.ops[eng].append((tuple(counts), None, None, 0))


def build_nc():
    nc = bass.Bass("TRN2", target_bir_lowering=False)
    din = {}

    def dI(name, shape, dt=F32):
        din[name] = nc.dram_tensor(name, list(shape), dt, kind="ExternalInput").ap()
        return din[name]

    xfull = dI("xfull", (S, D))
    xmine = dI("xmine", (TME, D))
    wq = dI("wq", (D, D)); wk = dI("wk", (D, KVD)); wv = dI("wv", (D, KVD))
    wo = dI("wo", (D, D))
    rw = dI("rw", (D, 1))
    rtw = dI("rtw", (D, E))
    wg = dI("wg", (E, D, F)); wu = dI("wu", (E, D, F)); wd = dI("wd", (E, F, D))
    swg = dI("swg", (D, SF)); swu = dI("swu", (D, SF)); swd = dI("swd", (SF, D))
    rkc = dI("rkc", (S, KVD)); rks = dI("rks", (S, KVD))
    rqc = dI("rqc", (TME, D)); rqs = dI("rqs", (TME, D))
    idb = dI("idb", (128, 128), BF16)
    idf = dI("idf", (128, 128), F32)
    dout = nc.dram_tensor("out", [TME, D], F32, kind="ExternalOutput").ap()

    kr_sp = nc.dram_tensor("kr_sp", [S, KVD], BF16, kind="Internal").ap()
    v_sp = nc.dram_tensor("v_sp", [NCH, 128, KVD], BF16, kind="Internal").ap()
    qr_sp = nc.dram_tensor("qr_sp", [TME, D], BF16, kind="Internal").ap()
    at_sp = nc.dram_tensor("at_sp", [NH, 128, TME], BF16, kind="Internal").ap()
    wf_sp = nc.dram_tensor("wf_sp", [18, NJ, 128, F], BF16, kind="Internal").ap()

    H_OFF = SBUF_TOP - 65536
    MOE_OFF = H_OFF - 65536
    galloc_off = [SBUF_BASE]

    def galloc(name, shape, dt, align=64):
        nb = {F32: 4, BF16: 2}[dt] * int(np.prod(shape[1:]))
        o = (galloc_off[0] + align - 1) // align * align
        galloc_off[0] = o + nb
        return nc.alloc_sbuf_tensor_at(name, list(shape), dt, offset=o)

    sid_b = galloc("sid_b", (128, 128), BF16)
    sid_f = galloc("sid_f", (128, 128), F32)
    ones_c = galloc("ones_c", (128, 1), BF16)
    ones_cf = galloc("ones_cf", (128, 1), F32)
    mean_c = galloc("mean_c", (128, 1), BF16)
    onesr = galloc("onesr", (1, 128), F32)
    onesr_b = galloc("onesr_b", (1, 128), BF16)
    cexpb = galloc("cexpb", (128, 1), F32)
    chn = galloc("chn", (1, 1), F32)
    chp = galloc("chp", (1, 1), F32)
    rw_sb = galloc("rw_sb", (128, NJ), F32)
    rsinv_row = galloc("rsinv_row", (1, TME), F32)
    GEND = (galloc_off[0] + 63) // 64 * 64

    def mkalloc(tag, limit=SBUF_TOP):
        st = [GEND]

        def alloc(name, shape, dt, align=64):
            nb = {F32: 4, BF16: 2}[dt] * int(np.prod(shape[1:]))
            o = (st[0] + align - 1) // align * align
            st[0] = o + nb
            assert st[0] <= limit, (tag, name, st[0], limit)
            return nc.alloc_sbuf_tensor_at(f"{tag}_{name}", list(shape), dt, offset=o)
        return alloc

    h_sb = nc.alloc_sbuf_tensor_at("h_sb", [128, NJ * 2048], BF16, offset=H_OFF)
    moe = nc.alloc_sbuf_tensor_at("moe", [128, NJ * 2048], BF16, offset=MOE_OFF)

    pA = nc.alloc_psum_tensor("pA", [128, 2048], F32)
    pB = nc.alloc_psum_tensor("pB", [128, 512], F32)
    pC = nc.alloc_psum_tensor("pC", [128, 512], F32)
    pD = nc.alloc_psum_tensor("pD", [128, 512], F32)
    pE = nc.alloc_psum_tensor("pE", [128, 512], F32)

    P = Plan()
    W = lambda *ms: [m for m in ms if m is not None]

    P.dma("sync", lambda e: e.dma_start(out=sid_b[:], in_=idb), "dId")
    id_ready = P.dma("sync", lambda e: e.dma_start(out=sid_f[:], in_=idf), "dId")
    rw_ld = P.dma("sync", lambda e: e.dma_start(
        out=rw_sb[:], in_=rw.rearrange("(j p) o -> p (j o)", p=128),
        allow_slow_non_contiguous=True), "dRw")
    P.op("vector", lambda e: e.memset(ones_c[:], 1.0))
    P.op("vector", lambda e: e.memset(ones_cf[:], 1.0))
    P.op("vector", lambda e: e.memset(mean_c[:], 1.0 / 2048.0))
    P.op("vector", lambda e: e.memset(onesr[:], 1.0))
    P.op("vector", lambda e: e.memset(onesr_b[:], 1.0))
    P.op("vector", lambda e: e.memset(cexpb[:], -SHIFT))
    P.op("vector", lambda e: e.memset(chn[:], -0.5))
    P.op("vector", lambda e: e.memset(chp[:], 0.5), inc="sVE")
    init_ve = P.ms("sVE")

    al = mkalloc("p1")
    wq_sb = al("wq", (128, NJ * 2048), BF16)
    wk_sb = al("wk", (128, NJ * 512), BF16)
    wv_sb = al("wv", (128, NJ * 512), BF16)
    xbf = [al(f"xbf{i}", (128, 2048), BF16) for i in range(2)]
    xt = [al(f"xt{i}", (128, 2048), BF16) for i in range(2)]
    kf = al("kf", (128, 512), F32)
    km1 = al("km1", (128, 512), F32)
    km2 = al("km2", (128, 512), F32)
    krot = [al(f"krot{i}", (128, 512), BF16) for i in range(2)]
    vt = [al(f"vt{i}", (128, 512), BF16) for i in range(2)]
    tbc = al("tbc", (128, 512), F32)
    tbs = al("tbs", (128, 512), F32)
    xf32 = [al(f"xf32{i}", (128, 2048), F32) for i in range(2)]
    qf = al("qf", (128, 2048), F32)
    qm1 = al("qm1", (128, 2048), F32)
    qm2 = al("qm2", (128, 512), F32)
    qrot = al("qrot", (128, 2048), BF16)
    wtmp = [al(f"wtmp{i}", (128, NJ * 128), BF16) for i in range(3)]
    rmf = al("rmf", (128, NJ * 128), BF16)

    P.op("vector", lambda e: e.memset(rmf[:], 1.0), inc="sVE")
    for j in range(NJ):
        P.op("vector", lambda e, j=j: e.tensor_scalar(
            rmf[:, j * 128:(j + 1) * 128], rmf[:, j * 128:(j + 1) * 128],
            rw_sb[:, j:j + 1], None, ALU.mult),
            waits=[rw_ld] if j == 0 else (), inc="sVE")
    rmf_ready = P.ms("sVE")

    P.dma("gpsimd", lambda e: e.dma_start(
        out=wk_sb[:].rearrange("p (j n) -> p j n", n=512),
        in_=wk.rearrange("(j p) n -> p j n", p=128)), "dWqkv")
    P.dma("gpsimd", lambda e: e.dma_start(
        out=wv_sb[:].rearrange("p (j n) -> p j n", n=512),
        in_=wv.rearrange("(j p) n -> p j n", p=128)), "dWqkv")
    w_ready = P.dma("gpsimd", lambda e: e.dma_start(
        out=wq_sb[:].rearrange("p (j n) -> p j n", n=2048),
        in_=wq.rearrange("(j p) n -> p j n", p=128)), "dWqkv")

    fold_jobs = []
    for m in range(18):
        src = wg[m] if m < 8 else (wu[m - 8] if m < 16 else (swg if m == 16 else swu))
        for sl in range(8):
            fold_jobs.append((m, sl, src))
    fold_i = [0]
    fold_busy = [None, None, None]

    def fold_step():
        i = fold_i[0]
        if i >= len(fold_jobs):
            return
        fold_i[0] = i + 1
        m, sl, src = fold_jobs[i]
        buf = i % 3
        wt = wtmp[buf]
        ld = P.dma("gpsimd", lambda e, wt=wt, src=src, sl=sl: e.dma_start(
            out=wt[:].rearrange("p (j f) -> p j f", f=128),
            in_=src[:, sl * 128:(sl + 1) * 128].rearrange("(j p) f -> p j f", p=128)),
            f"dWtmp{buf}", waits=W(fold_busy[buf]))
        folded = P.op("vector", lambda e, wt=wt: e.tensor_tensor(
            wt[:], wt[:], rmf[:], ALU.mult), waits=[ld, rmf_ready])
        fold_busy[buf] = P.dma("sync", lambda e, wt=wt, m=m, sl=sl: e.dma_start(
            out=wf_sp[m, :, :, sl * 128:(sl + 1) * 128].transpose([1, 0, 2]),
            in_=wt[:].rearrange("p (j f) -> p j f", f=128)),
            f"dFold{buf}", waits=[folded])

    xbf_free = [None, None]
    xt_free = [None, None]
    pD_free = None
    pE_free = None
    pK_free = None
    pV_free = None
    pA_free = None
    kf_free = None
    qf_free = None
    qrot_free = [None]
    krot_free = [None, None]
    vt_free = [None, None]
    tb_free = None
    kt_spilled = None
    qt_spilled = None
    strip_evs = []
    qstrip_evs = []

    pA_free_p1 = [None]

    xt_rdy = {}
    cast_rdy = {}
    xf32_free = [None, None]

    def load_cast(c):
        mine = c >= NCH
        cc = c - NCH if mine else c
        buf = c % 2
        src = (xmine if mine else xfull)[cc * 128:(cc + 1) * 128, :]
        x_ld = P.dma("scalar", lambda e, b=buf, src=src: e.dma_start(
            out=xf32[b][:], in_=src), f"dXf{buf}", waits=W(xf32_free[buf]))
        P.op("gpsimd", lambda e, b=buf: e.tensor_copy(xbf[b][:], xf32[b][:]),
             waits=[x_ld] + W(xbf_free[buf]))
        cast_rdy[buf] = P.ms("sPO")
        xf32_free[buf] = cast_rdy[buf]

    def prep_t(c):
        nonlocal pD_free, pE_free
        buf = c % 2
        for j in range(NJ):
            tgt, colo = (pD, j * 64) if j < 8 else (pE, (j - 8) * 64)
            wt0 = W(cast_rdy[buf], pD_free, pE_free, init_ve, id_ready) if j == 0 else ()
            P.op("tensor", lambda e, b=buf, j=j, tgt=tgt, colo=colo: e.transpose(
                tgt[:, colo:colo + 64].bitcast(BF16),
                xbf[b][:, j * 128:(j + 1) * 128], sid_b[:]),
                waits=wt0, inc="sPE" if j in (7, 15) else None)
            if j == 7:
                tpD = P.ms("sPE")
        tpE = P.ms("sPE")
        xbf_free[buf] = tpE
        P.op("scalar", lambda e, b=buf: e.copy(xt[b][:, 0:1024], pD[:].bitcast(BF16)),
             waits=[tpD] + W(xt_free[buf]), inc="sSE")
        xt_r1 = P.ms("sSE")
        P.op("scalar", lambda e, b=buf: e.copy(xt[b][:, 1024:2048], pE[:].bitcast(BF16)),
             waits=[tpE], inc="sSE")
        xt_ready = P.ms("sSE")
        pD_free = xt_ready
        pE_free = xt_ready
        xt_rdy[buf] = (xt_r1, xt_ready)

    load_cast(0)
    load_cast(1)
    prep_t(0)
    for c in range(NCH + NMY):
        mine = c >= NCH
        cc = c - NCH if mine else c
        buf = c % 2
        xt_r1, xt_ready = xt_rdy[buf]

        if not mine:
            for j in range(NJ):
                wj = ()
                if j == 0:
                    wj = [xt_r1, w_ready] + W(pK_free)
                elif j == 8:
                    wj = [xt_ready]
                P.op("tensor", lambda e, b=buf, j=j: e.matmul(
                    pB[:], xt[b][:, j * 128:(j + 1) * 128],
                    wk_sb[:, j * 512:(j + 1) * 512],
                    start=(j == 0), stop=(j == NJ - 1)),
                    waits=wj, inc="sPE" if j == NJ - 1 else None)
            k_done = P.ms("sPE")
            if c + 2 < NCH + NMY:
                load_cast(c + 2)
            if c + 1 < NCH + NMY:
                prep_t(c + 1)
            for j in range(NJ):
                P.op("tensor", lambda e, b=buf, j=j: e.matmul(
                    pC[:], xt[b][:, j * 128:(j + 1) * 128],
                    wv_sb[:, j * 512:(j + 1) * 512],
                    start=(j == 0), stop=(j == NJ - 1)),
                    waits=W(pV_free) if j == 0 else (),
                    inc="sPE" if j == NJ - 1 else None)
            kv_done = P.ms("sPE")
            xt_free[buf] = kv_done
            P.op("scalar", lambda e: e.copy(kf[:], pB[:]),
                 waits=[k_done] + W(kf_free), inc="sSE")
            kf_ready = P.ms("sSE")
            pK_free = kf_ready
            P.op("scalar", lambda e, b=buf: e.copy(vt[b][:], pC[:]),
                 waits=[kv_done] + W(vt_free[buf]), inc="sSE")
            v_ready = P.ms("sSE")
            pV_free = v_ready
            vt_free[buf] = P.dma("sync", lambda e, b=buf, c=c: e.dma_start(
                out=v_sp[c], in_=vt[b][:]), f"dVsp{buf}", waits=[v_ready])
            P.dma("sync", lambda e, cc=cc: e.dma_start(
                out=tbc[:], in_=rkc[cc * 128:(cc + 1) * 128, :]),
                "dTb", waits=W(tb_free))
            tb_ready = P.dma("sync", lambda e, cc=cc: e.dma_start(
                out=tbs[:], in_=rks[cc * 128:(cc + 1) * 128, :]), "dTb")
            P.op("vector", lambda e: e.tensor_tensor(km1[:], kf[:], tbc[:], ALU.mult),
                 waits=[kf_ready, tb_ready], inc="sVE")
            P.op("vector", lambda e: e.tensor_tensor(
                km2[:].rearrange("p (h s x) -> p h s x", h=4, s=2)[:, :, 0, :],
                kf[:].rearrange("p (h s x) -> p h s x", h=4, s=2)[:, :, 1, :],
                tbs[:].rearrange("p (h s x) -> p h s x", h=4, s=2)[:, :, 0, :],
                ALU.mult), inc="sVE")
            P.op("vector", lambda e: e.tensor_tensor(
                km2[:].rearrange("p (h s x) -> p h s x", h=4, s=2)[:, :, 1, :],
                kf[:].rearrange("p (h s x) -> p h s x", h=4, s=2)[:, :, 0, :],
                tbs[:].rearrange("p (h s x) -> p h s x", h=4, s=2)[:, :, 1, :],
                ALU.mult), inc="sVE")
            P.op("vector", lambda e, b=buf: e.tensor_tensor(
                krot[b][:], km1[:], km2[:], ALU.add),
                waits=W(krot_free[buf]), inc="sVE")
            rope_done = P.ms("sVE")
            kf_free = rope_done
            tb_free = rope_done
            krot_free[buf] = P.dma("sync", lambda e, b=buf, c=c: e.dma_start(
                out=kr_sp[c * 128:(c + 1) * 128, :], in_=krot[b][:]),
                f"dKr{buf}", waits=[rope_done])
        else:
            for qb in range(4):
                for j in range(NJ):
                    P.op("tensor", lambda e, b=buf, j=j, qb=qb: e.matmul(
                        pA[:, qb * 512:(qb + 1) * 512],
                        xt[b][:, j * 128:(j + 1) * 128],
                        wq_sb[:, j * 2048 + qb * 512: j * 2048 + (qb + 1) * 512],
                        start=(j == 0), stop=(j == NJ - 1)),
                        waits=([xt_r1, w_ready] + W(pA_free_p1[0]))
                              if (qb == 0 and j == 0) else
                              ([xt_ready] if (qb == 0 and j == 8) else ()),
                        inc="sPE" if (qb == 3 and j == NJ - 1) else None)
            q_done = P.ms("sPE")
            xt_free[buf] = q_done
            P.op("scalar", lambda e: e.copy(qf[:, 0:1024], pA[:, 0:1024]),
                 waits=[q_done] + W(qf_free), inc="sSE")
            P.op("scalar", lambda e: e.copy(qf[:, 1024:2048], pA[:, 1024:2048]), inc="sSE")
            qf_ready = P.ms("sSE")
            pA_free_p1[0] = qf_ready
            if c + 2 < NCH + NMY:
                load_cast(c + 2)
            if c + 1 < NCH + NMY:
                prep_t(c + 1)
            last_ms = None
            for si in range(4):
                P.dma("sync", lambda e, cc=cc, si=si: e.dma_start(
                    out=tbc[:], in_=rqc[cc * 128:(cc + 1) * 128, si * 512:(si + 1) * 512]),
                    "dTb", waits=W(tb_free))
                tb_ready = P.dma("sync", lambda e, cc=cc, si=si: e.dma_start(
                    out=tbs[:], in_=rqs[cc * 128:(cc + 1) * 128, si * 512:(si + 1) * 512]),
                    "dTb")
                sl = slice(si * 512, (si + 1) * 512)
                P.op("vector", lambda e, sl=sl: e.tensor_tensor(
                    qm1[:, sl], qf[:, sl], tbc[:], ALU.mult),
                    waits=[qf_ready, tb_ready], inc="sVE")
                P.op("vector", lambda e, sl=sl: e.tensor_tensor(
                    qm2[:].rearrange("p (h s x) -> p h s x", h=4, s=2)[:, :, 0, :],
                    qf[:, sl].rearrange("p (h s x) -> p h s x", h=4, s=2)[:, :, 1, :],
                    tbs[:].rearrange("p (h s x) -> p h s x", h=4, s=2)[:, :, 0, :],
                    ALU.mult), inc="sVE")
                P.op("vector", lambda e, sl=sl: e.tensor_tensor(
                    qm2[:].rearrange("p (h s x) -> p h s x", h=4, s=2)[:, :, 1, :],
                    qf[:, sl].rearrange("p (h s x) -> p h s x", h=4, s=2)[:, :, 0, :],
                    tbs[:].rearrange("p (h s x) -> p h s x", h=4, s=2)[:, :, 1, :],
                    ALU.mult), inc="sVE")
                P.op("vector", lambda e, sl=sl: e.tensor_tensor(
                    qrot[:, sl], qm1[:, sl], qm2[:], ALU.add),
                    waits=W(qrot_free[0]) if si == 0 else (), inc="sVE")
                last_ms = P.ms("sVE")
                tb_free = last_ms
            qrope_done = last_ms
            qf_free = qrope_done
            qrot_free[0] = P.dma("sync", lambda e, cc=cc: e.dma_start(
                out=qr_sp[cc * 128:(cc + 1) * 128, :], in_=qrot[:]),
                "dQr", waits=[qrope_done])
        fold_step()
        fold_step()
    while fold_i[0] < len(fold_jobs):
        fold_step()

    P.barrier()

    al = mkalloc("p2")
    ktb = [al(f"ktb{i}", (128, S), BF16) for i in range(2)]
    vtb = [al(f"vtb{i}", (128, NCH * 128), BF16) for i in range(2)]
    qtb = [al(f"qtb{i}", (128, TME), BF16) for i in range(4)]
    expS = [al(f"expS{i}", (128, 2048), BF16) for i in range(2)]
    sumacc = al("sumacc", (128, 2048), F32)
    sumacc2 = al("sumacc2", (128, 2048), F32)
    av_f = [al(f"avf{i}", (128, 512), F32) for i in range(2)]
    rep_f = [al(f"repf{i}", (128, 512), F32) for i in range(2)]
    att_t = [al(f"att{i}", (128, 512), BF16) for i in range(4)]
    d_row = [al(f"drow{i}", (1, 512), F32) for i in range(2)]

    kv_free = [None, None]
    kv_ready = [None, None]
    qt_free = [None] * 4
    qt_ready = [None] * 4
    expS_free = [None, None]
    att_free = [None] * 4
    avf_free = [None, None]
    repf_free = [None, None]
    drow_free = [None, None]

    def prefetch_kv(g):
        b = g % 2
        P.dma("sync", lambda e, g=g, b=b: e.dma_start(
            out=ktb[b][:], in_=kr_sp[:, g * 128:(g + 1) * 128], transpose=True),
            f"dKv{b}", waits=W(kv_free[b]))
        kv_ready[b] = P.dma("sync", lambda e, g=g, b=b: e.dma_start(
            out=vtb[b][:].rearrange("p (c x) -> p c x", x=128),
            in_=v_sp[:, :, g * 128:(g + 1) * 128].transpose([1, 0, 2])), f"dKv{b}")

    def prefetch_q(h, slot):
        qt_ready[slot] = P.dma("sync", lambda e, h=h, slot=slot: e.dma_start(
            out=qtb[slot][:], in_=qr_sp[:, h * 128:(h + 1) * 128], transpose=True),
            f"dQs{slot}", waits=W(qt_free[slot]))

    prefetch_kv(0)
    prefetch_q(0, 0)
    prefetch_q(1, 1)
    pA_free2 = None
    exph_ms = [None, None]
    pDE_free = [None]
    sumacc_free = [None]
    pBC_free2 = [None, None]
    pend_epi = []

    def mk_epi(sum_done_v, sum_done_p, avf0, avf1, h0, h1, qt):
        box = {'avf': [avf0, avf1]}

        def s1():
            for hi in range(2):
                for k2 in range(2):
                    P.op("tensor", lambda e, hi=hi, k2=k2: e.matmul(
                        (pD if hi == 0 else pE)[0:1, :], ones_cf[:],
                        sumacc[:, hi * 1024 + k2 * 512: hi * 1024 + (k2 + 1) * 512],
                        start=(k2 == 0), stop=(k2 == 1)),
                        waits=([sum_done_v] + W(pDE_free[0]))
                              if (hi == 0 and k2 == 0) else (),
                        inc="sPE" if (hi == 1 and k2 == 1) else None)
            box['sums'] = P.ms("sPE")
            sumacc_free[0] = box['sums']

        def s2():
            P.op("vector", lambda e: e.tensor_copy(d_row[0][:], pD[0:1, :]),
                 waits=[box['sums']] + W(drow_free[0], drow_free[1]))
            P.op("vector", lambda e: e.tensor_copy(d_row[1][:], pE[0:1, :]))
            P.op("vector", lambda e: e.reciprocal(d_row[0][:], d_row[0][:]))
            P.op("vector", lambda e: e.reciprocal(d_row[1][:], d_row[1][:]))
            box['rcp'] = P.ms("sVE")

        def s3():
            P.op("tensor", lambda e: e.matmul(
                pD[:], onesr[:], d_row[0][:], start=True, stop=True),
                waits=W(box['rcp']))
            P.op("tensor", lambda e: e.matmul(
                pE[:], onesr[:], d_row[1][:], start=True, stop=True),
                inc="sPE")
            box['reps'] = P.ms("sPE")
            drow_free[0] = box['reps']
            drow_free[1] = box['reps']

        def s4():
            for hi in range(2):
                P.op("scalar", lambda e, hi=hi: e.copy(
                    rep_f[hi][:], (pD if hi == 0 else pE)[:]),
                    waits=([box['reps']] if hi == 0 else []) + W(repf_free[hi]),
                    inc="sSE")
            box['evs'] = P.ms("sSE")
            pDE_free[0] = box['evs']

        def s5():
            for hi, hh in enumerate((h0, h1)):
                slot = (qt * 2 + hi) % 4
                P.op("vector", lambda e, hi=hi, slot=slot: e.tensor_tensor(
                    att_t[slot][:], av_f[hi][:], rep_f[hi][:], ALU.mult),
                    waits=[box['evs'], box['avf'][hi]] + W(att_free[slot]))
                nrm = P.ms("sVE")
                avf_free[hi] = nrm
                repf_free[hi] = nrm
                att_free[slot] = P.dma("sync",
                    lambda e, hh=hh, qt=qt, slot=slot: e.dma_start(
                        out=at_sp[hh][:, qt * 512:(qt + 1) * 512],
                        in_=att_t[slot][:]),
                    f"dAt{slot}", waits=[nrm])

        return [(1, s1), (3, s2), (5, s3), (7, s4), (9, s5)]

    for g in range(NKV):
        gb = g % 2
        if g + 1 < NKV:
            prefetch_kv(g + 1)
        for pr in range(2):
            pi = g * 2 + pr
            h0, h1 = 4 * g + 2 * pr, 4 * g + 2 * pr + 1
            s0, s1 = (2 * pi) % 4, (2 * pi + 1) % 4
            if pi + 1 < 8:
                prefetch_q(h0 + 2 if pr == 0 else 4 * (g + 1), (s0 + 2) % 4)
                prefetch_q((h0 + 3) if pr == 0 else 4 * (g + 1) + 1, (s1 + 2) % 4)
            for qt in range(NQT):
                av_ms = [None, None]
                for kk in range(32):
                    kc = kk * 2
                    eb = expS[(kk - 1) % 2]
                    kcp = (kk - 1) * 2
                    first = (kk - 1 == 0)
                    while pend_epi and kk >= pend_epi[0][0]:
                        pend_epi.pop(0)[1]()
                    if kk >= 1:
                        seng = "vector"
                        sacc = sumacc
                        if kk - 1 == 0:
                            P.op(seng, lambda e, eb=eb, sacc=sacc: e.tensor_copy(
                                sacc[:], eb[:]),
                                waits=W(exph_ms[0], exph_ms[1], sumacc_free[0]))
                        else:
                            P.op(seng, lambda e, eb=eb, sacc=sacc: e.tensor_tensor(
                                sacc[:], sacc[:], eb[:], ALU.add),
                                waits=W(exph_ms[0], exph_ms[1]))
                        sum_ms_v = P.ms("sVE")
                    for hi, ss in ((0, s0), (1, s1)):
                        w0 = W(exph_ms[hi])
                        if kk == 0:
                            w0 += W(kv_ready[gb], qt_ready[ss], pA_free2)
                        for k2 in range(2):
                            P.op("tensor", lambda e, gb=gb, kc=kc, k2=k2, ss=ss, hi=hi, qt=qt: e.matmul(
                                pA[:, hi * 1024 + k2 * 512: hi * 1024 + (k2 + 1) * 512],
                                ktb[gb][:, (kc + k2) * 128:(kc + k2 + 1) * 128],
                                qtb[ss][:, qt * 512:(qt + 1) * 512],
                                start=True, stop=True),
                                waits=w0 if k2 == 0 else (),
                                inc="sPE" if k2 == 1 else None)
                        sh_ms = P.ms("sPE")
                        if kk >= 1:
                            for k2 in range(2):
                                P.op("tensor", lambda e, gb=gb, kcp=kcp, k2=k2, hi=hi, eb=eb, first=first: e.matmul(
                                    (pB if hi == 0 else pC)[:],
                                    vtb[gb][:, (kcp + k2) * 128:(kcp + k2 + 1) * 128],
                                    eb[:, hi * 1024 + k2 * 512: hi * 1024 + (k2 + 1) * 512],
                                    start=(first and k2 == 0), stop=False),
                                    waits=W(pBC_free2[hi]) if (first and k2 == 0) else (),
                                    inc="sPE" if k2 == 1 else None)
                            av_ms[hi] = P.ms("sPE")
                        we = [sh_ms]
                        if hi == 0 and expS_free[kk % 2]:
                            we += expS_free[kk % 2]
                        P.op("scalar", lambda e, kk=kk, hi=hi: e.activation(
                            expS[kk % 2][:, hi * 1024:(hi + 1) * 1024],
                            pA[:, hi * 1024:(hi + 1) * 1024], AF.Exp, bias=cexpb[:]),
                            waits=we, inc="sSE")
                        exph_ms[hi] = P.ms("sSE")
                    if kk >= 1:
                        expS_free[(kk - 1) % 2] = [av_ms[0], av_ms[1], sum_ms_v]
                eb = expS[31 % 2]
                kcp = 62
                for hi in range(2):
                    for k2 in range(2):
                        P.op("tensor", lambda e, gb=gb, kcp=kcp, k2=k2, hi=hi, eb=eb: e.matmul(
                            (pB if hi == 0 else pC)[:],
                            vtb[gb][:, (kcp + k2) * 128:(kcp + k2 + 1) * 128],
                            eb[:, hi * 1024 + k2 * 512: hi * 1024 + (k2 + 1) * 512],
                            start=False, stop=(k2 == 1)),
                            waits=[exph_ms[hi]] if k2 == 0 else (),
                            inc="sPE" if k2 == 1 else None)
                    av_ms[hi] = P.ms("sPE")
                P.op("vector", lambda e, eb=eb: e.tensor_tensor(
                    sumacc[:], sumacc[:], eb[:], ALU.add),
                    waits=W(exph_ms[0], exph_ms[1]))
                sum_done_v = P.ms("sVE")
                sum_done_p = P.ms("sPO")
                expS_free[1] = [av_ms[0], av_ms[1], sum_done_v]
                avf_in = [None, None]
                for hi in range(2):
                    P.op("scalar", lambda e, hi=hi: e.copy(
                        av_f[hi][:], pB[:] if hi == 0 else pC[:]),
                        waits=[av_ms[hi]] + W(avf_free[hi]), inc="sSE")
                    avf_in[hi] = P.ms("sSE")
                    pBC_free2[hi] = avf_in[hi]
                pend_epi.extend(mk_epi(sum_done_v, sum_done_p,
                                       avf_in[0], avf_in[1], h0, h1, qt))
            pairdone = P.ms("sPE")
            qt_free[s0] = pairdone
            qt_free[s1] = pairdone
        kv_free[gb] = P.ms("sPE")
    while pend_epi:
        pend_epi.pop(0)[1]()

    P.barrier()

    al = mkalloc("p3", limit=H_OFF)
    wo_sb = al("wo", (128, NJ * 2048), BF16)
    abuf = al("abuf", (128, NH * 512), BF16)
    x1pre = al("x1pre", (128, NJ * 512), BF16)
    xres = [al(f"xres{i}", (128, 2048), F32) for i in range(2)]
    wo_f = [al(f"wof{i}", (128, 512), F32) for i in range(2)]
    sq_t = al("sq", (128, 512), BF16)
    mrep_f = al("mrep", (128, 512), F32)
    rxrep_f = al("rxrep", (128, 512), F32)
    rows = al("rows", (1, 6 * 512), F32)
    r_m = rows[0:1, 0:512]; r_s = rows[0:1, 512:1024]; r_t = rows[0:1, 1024:1536]
    r_u = rows[0:1, 1536:2048]; r_q = rows[0:1, 2048:2560]; r_r = rows[0:1, 2560:3072]

    wo_ready = P.dma("gpsimd", lambda e: e.dma_start(
        out=wo_sb[:].rearrange("p (i n) -> p i n", n=2048),
        in_=wo.rearrange("(i p) n -> p i n", p=128)), "dWo")

    abuf_free = None
    xres_free = [None, None]
    pB_free = None
    pA_free3 = None
    wof_free = [None, None]
    mrep_free = None
    sq_free = None
    x1pre_free = None

    for qt in range(NQT):
        a_ready = P.dma("sync", lambda e, qt=qt: e.dma_start(
            out=abuf[:].rearrange("p (i x) -> p i x", x=512),
            in_=at_sp[:, :, qt * 512:(qt + 1) * 512].transpose([1, 0, 2])),
            "dAbuf", waits=W(abuf_free))
        x1_init = []
        for tc in range(4):
            tb = tc % 2
            xr = P.dma("sync", lambda e, qt=qt, tc=tc, tb=tb: e.dma_start(
                out=xres[tb][:],
                in_=xmine[(qt * 4 + tc) * 128:(qt * 4 + tc + 1) * 128, :]),
                f"dXr{tb}", waits=W(xres_free[tb]))
            for j in range(NJ):
                P.op("tensor", lambda e, j=j, tb=tb: e.transpose(
                    pA[:, j * 128:(j + 1) * 128], xres[tb][:, j * 128:(j + 1) * 128],
                    sid_f[:]),
                    waits=([xr] + W(pA_free3)) if j == 0 else (),
                    inc="sPE" if j == NJ - 1 else None)
            tp = P.ms("sPE")
            xres_free[tb] = tp
            P.op("scalar", lambda e, tc=tc: e.copy(
                x1pre[:].rearrange("p (j x) -> p j x", x=512)[:, :, tc * 128:(tc + 1) * 128],
                pA[:].rearrange("p (j x) -> p j x", x=128)),
                waits=[tp] + (W(x1pre_free) if tc == 0 else []), inc="sSE")
            pA_free3 = P.ms("sSE")
            x1_init.append(pA_free3)
        stats_start = None
        for j in range(NJ):
            jb = j % 2
            for i in range(NJ):
                P.op("tensor", lambda e, i=i, j=j: e.matmul(
                    pB[:], wo_sb[:, i * 2048 + j * 128: i * 2048 + (j + 1) * 128],
                    abuf[:, i * 512:(i + 1) * 512],
                    start=(i == 0), stop=(i == NJ - 1)),
                    waits=([a_ready, wo_ready] + W(pB_free)) if i == 0 else (),
                    inc="sPE" if i == NJ - 1 else None)
            mm = P.ms("sPE")
            P.op("scalar", lambda e, jb=jb: e.copy(wo_f[jb][:], pB[:]),
                 waits=[mm] + W(wof_free[jb]), inc="sSE")
            pB_free = P.ms("sSE")
            P.op("vector", lambda e, j=j, jb=jb: e.tensor_tensor(
                x1pre[:, j * 512:(j + 1) * 512], x1pre[:, j * 512:(j + 1) * 512],
                wo_f[jb][:], ALU.add),
                waits=[pB_free] + (x1_init if j == 0 else []), inc="sVE")
            x1_ms = P.ms("sVE")
            wof_free[jb] = x1_ms
            P.op("vector", lambda e, j=j: e.tensor_tensor(
                sq_t[:], x1pre[:, j * 512:(j + 1) * 512],
                x1pre[:, j * 512:(j + 1) * 512], ALU.mult),
                waits=W(sq_free), inc="sVE")
            sq_ms = P.ms("sVE")
            P.op("tensor", lambda e, j=j: e.matmul(
                pD[0:1, :], mean_c[:], x1pre[:, j * 512:(j + 1) * 512],
                start=(j == 0), stop=(j == NJ - 1)),
                waits=[x1_ms] + (W(mrep_free) if j == 0 else []))
            P.op("tensor", lambda e, j=j: e.matmul(
                pE[0:1, :], mean_c[:], sq_t[:],
                start=(j == 0), stop=(j == NJ - 1)), waits=[sq_ms],
                inc="sPE")
            sq_free = P.ms("sPE")
        stats_done = P.ms("sPE")
        abuf_free = stats_done
        P.op("scalar", lambda e: e.copy(r_m, pD[0:1, :]), waits=[stats_done], inc="sSE")
        P.op("scalar", lambda e: e.copy(r_s, pE[0:1, :]), inc="sSE")
        rows_ms = P.ms("sSE")
        P.op("vector", lambda e: e.tensor_tensor(r_t, r_m, r_m, ALU.mult),
             waits=[rows_ms], inc="sVE")
        P.op("vector", lambda e: e.tensor_tensor(r_s, r_s, r_t, ALU.subtract), inc="sVE")
        P.op("vector", lambda e: e.tensor_scalar(r_u, r_s, 1e-5, None, ALU.add), inc="sVE")
        P.op("vector", lambda e: e.reciprocal(r_t, r_u), inc="sVE")
        P.op("vector", lambda e: e.tensor_tensor(r_q, r_s, r_t, ALU.mult), inc="sVE")
        P.op("vector", lambda e: e.tensor_scalar(r_q, r_q, 1e-6, None, ALU.add), inc="sVE")
        vr = P.ms("sVE")
        P.op("scalar", lambda e: e.activation(r_u, r_u, AF.Ln), waits=[vr], inc="sSE")
        P.op("scalar", lambda e: e.activation(r_r, r_u, AF.Exp, scale=chn[0:1, 0:1]), inc="sSE")
        P.op("scalar", lambda e: e.activation(r_q, r_q, AF.Ln), inc="sSE")
        P.op("scalar", lambda e: e.activation(r_t, r_q, AF.Exp, scale=chn[0:1, 0:1]), inc="sSE")
        P.op("scalar", lambda e, qt=qt: e.activation(
            rsinv_row[0:1, qt * 512:(qt + 1) * 512], r_q, AF.Exp, scale=chp[0:1, 0:1]),
            inc="sSE")
        srow = P.ms("sSE")
        P.op("vector", lambda e: e.tensor_tensor(r_r, r_r, r_t, ALU.mult),
             waits=[srow], inc="sVE")
        rxm = P.ms("sVE")
        P.op("tensor", lambda e: e.matmul(pD[:], onesr[:], r_m, start=True, stop=True),
             waits=[rxm])
        P.op("tensor", lambda e: e.matmul(pE[:], onesr[:], r_r, start=True, stop=True),
             inc="sPE")
        repd = P.ms("sPE")
        P.op("scalar", lambda e: e.copy(mrep_f[:], pD[:]),
             waits=[repd] + W(x1pre_free), inc="sSE")
        P.op("scalar", lambda e: e.copy(rxrep_f[:], pE[:]), inc="sSE")
        repev = P.ms("sSE")
        mrep_free = repev
        for j in range(NJ):
            jb = j % 2
            P.op("vector", lambda e, j=j, jb=jb: e.tensor_tensor(
                wo_f[jb][:], x1pre[:, j * 512:(j + 1) * 512], mrep_f[:], ALU.subtract),
                waits=[repev] + W(wof_free[jb]), inc="sVE")
            P.op("vector", lambda e, j=j, jb=jb, qt=qt: e.tensor_tensor(
                h_sb[:, j * 2048 + qt * 512: j * 2048 + (qt + 1) * 512],
                wo_f[jb][:], rxrep_f[:], ALU.mult), inc="sVE")
            wof_free[jb] = P.ms("sVE")
        x1pre_free = P.ms("sVE")
    h_done = P.ms("sVE")

    P.barrier()

    al = mkalloc("p4", limit=MOE_OFF)
    act = al("act", (128, NFC * 2048), BF16)
    wgu = [al(f"wgu{i}", (128, NJ * 128), BF16) for i in range(3)]
    wdt = [al(f"wdt{i}", (128, NFC * 128), BF16) for i in range(2)]
    wrep = al("wrep", (128, 2048), BF16)
    u_t = al("ut", (128, 2048), BF16)
    d_f = [al(f"df{i}", (128, 512), F32) for i in range(1)]
    rtw_sb = al("rtw", (128, NJ * 8), BF16)
    lg_t = al("lg", (8, 512), F32)
    lgn = al("lgn", (128, 32), F32)
    lge = al("lge", (128, 32), F32)
    weq = al("weq", (128, 32), F32)
    wnat = al("wnat", (128, 32), F32)
    wmax = al("wmax", (128, 4), F32)
    wsum = al("wsum", (128, 4), F32)
    wv1 = al("wv1", (128, 4), F32)
    wv2 = al("wv2", (128, 4), F32)
    wden = al("wden", (128, 4), F32)
    wT_s = al("wTs", (8, TME), BF16)
    wrow = al("wrow", (1, 512), BF16)

    rt_ld = P.dma("gpsimd", lambda e: e.dma_start(
        out=rtw_sb[:].rearrange("p (j n) -> p j n", n=8),
        in_=rtw.rearrange("(j p) n -> p j n", p=128),
        allow_slow_non_contiguous=True), "dRtw")
    for j in range(NJ):
        P.op("vector", lambda e, j=j: e.tensor_scalar(
            rtw_sb[:, j * 8:(j + 1) * 8], rtw_sb[:, j * 8:(j + 1) * 8],
            rw_sb[:, j:j + 1], None, ALU.mult),
            waits=[rt_ld] if j == 0 else (), inc="sVE")
    rt_folded = P.ms("sVE")
    wT_done = []
    lg_free = None
    for qt in range(NQT):
        for j in range(NJ):
            P.op("tensor", lambda e, j=j, qt=qt: e.matmul(
                pD[0:8, :], rtw_sb[:, j * 8:(j + 1) * 8],
                h_sb[:, j * 2048 + qt * 512: j * 2048 + (qt + 1) * 512],
                start=(j == 0), stop=(j == NJ - 1)),
                waits=([rt_folded] + W(lg_free)) if j == 0 else (),
                inc="sPE" if j == NJ - 1 else None)
        lg_done = P.ms("sPE")
        P.op("scalar", lambda e: e.copy(lg_t[:], pD[0:8, :]),
             waits=[lg_done] + W(lg_free), inc="sSE")
        lg_ev = P.ms("sSE")
        for tc in range(4):
            P.op("tensor", lambda e, tc=tc: e.transpose(
                pE[:, tc * 8:(tc + 1) * 8], lg_t[0:8, tc * 128:(tc + 1) * 128], sid_f[0:8, 0:8]),
                waits=[lg_ev] if tc == 0 else (), inc="sPE" if tc == 3 else None)
        tpd = P.ms("sPE")
        P.op("scalar", lambda e: e.copy(lgn[:], pE[:, 0:32]), waits=[tpd], inc="sSE")
        lgn_ev = P.ms("sSE")
        P.op("vector", lambda e: e.tensor_reduce(
            wmax[:], lgn[:].rearrange("p (t x) -> p t x", x=8), AX.X, ALU.max),
            waits=[lgn_ev], inc="sVE")
        for tc in range(4):
            P.op("vector", lambda e, tc=tc: e.tensor_scalar(
                lgn[:, tc * 8:(tc + 1) * 8], lgn[:, tc * 8:(tc + 1) * 8],
                wmax[:, tc:tc + 1], None, ALU.subtract), inc="sVE")
        sub_ms = P.ms("sVE")
        P.op("scalar", lambda e: e.activation(lge[:], lgn[:], AF.Exp),
             waits=[sub_ms], inc="sSE")
        exp_ms = P.ms("sSE")
        P.op("vector", lambda e: e.tensor_reduce(
            wsum[:], lge[:].rearrange("p (t x) -> p t x", x=8), AX.X, ALU.add),
            waits=[exp_ms], inc="sVE")
        P.op("vector", lambda e: e.reciprocal(wsum[:], wsum[:]), inc="sVE")
        for tc in range(4):
            P.op("vector", lambda e, tc=tc: e.tensor_scalar(
                lge[:, tc * 8:(tc + 1) * 8], lge[:, tc * 8:(tc + 1) * 8],
                wsum[:, tc:tc + 1], None, ALU.mult), inc="sVE")
        P.op("vector", lambda e: e.tensor_reduce(
            wv1[:], lge[:].rearrange("p (t x) -> p t x", x=8), AX.X, ALU.max), inc="sVE")
        for tc in range(4):
            P.op("vector", lambda e, tc=tc: e.tensor_scalar(
                weq[:, tc * 8:(tc + 1) * 8], lge[:, tc * 8:(tc + 1) * 8],
                wv1[:, tc:tc + 1], None, ALU.is_equal), inc="sVE")
        P.op("vector", lambda e: e.tensor_scalar(weq[:], weq[:], 1e9, None, ALU.mult),
             inc="sVE")
        P.op("vector", lambda e: e.tensor_tensor(weq[:], lge[:], weq[:], ALU.subtract),
             inc="sVE")
        P.op("vector", lambda e: e.tensor_reduce(
            wv2[:], weq[:].rearrange("p (t x) -> p t x", x=8), AX.X, ALU.max), inc="sVE")
        P.op("vector", lambda e: e.tensor_tensor(wden[:], wv1[:], wv2[:], ALU.add), inc="sVE")
        P.op("vector", lambda e: e.reciprocal(wden[:], wden[:]), inc="sVE")
        for tc in range(4):
            P.op("vector", lambda e, tc=tc: e.tensor_scalar(
                wnat[:, tc * 8:(tc + 1) * 8], lge[:, tc * 8:(tc + 1) * 8],
                wv2[:, tc:tc + 1], None, ALU.is_ge), inc="sVE")
        P.op("vector", lambda e: e.tensor_tensor(wnat[:], wnat[:], lge[:], ALU.mult),
             inc="sVE")
        for tc in range(4):
            P.op("vector", lambda e, tc=tc: e.tensor_scalar(
                wnat[:, tc * 8:(tc + 1) * 8], wnat[:, tc * 8:(tc + 1) * 8],
                wden[:, tc:tc + 1], None, ALU.mult), inc="sVE")
        w_ms = P.ms("sVE")
        for tc in range(4):
            P.op("tensor", lambda e, tc=tc: e.transpose(
                pD[0:8, tc * 128:(tc + 1) * 128], wnat[:, tc * 8:(tc + 1) * 8], sid_f[:]),
                waits=[w_ms, lg_ev] if tc == 0 else (), inc="sPE" if tc == 3 else None)
        wtp = P.ms("sPE")
        P.op("scalar", lambda e, qt=qt: e.copy(
            wT_s[0:8, qt * 512:(qt + 1) * 512], pD[0:8, :]), waits=[wtp], inc="sSE")
        lg_free = P.ms("sSE")
        wT_done.append(lg_free)

    UPB = (pB, pC, pD, pE)
    ucp_free = None
    gu_ld = [None] * 4
    gu_free = [None] * 4
    wdt_free = [None, None]
    wdt_ld = [None, None]
    act_free_fc = [None] * NFC
    ut_free = None
    df_free = [None]
    moe_w = []
    wrep_free = None
    wrow_free = None
    prev_act_ms = None
    pA_free4 = None
    pBC_free = [None, None]

    for eidx in range(9):
        shared = (eidx == 0)
        ex = eidx - 1
        mg = 16 if shared else ex
        mu = 17 if shared else 8 + ex
        wdsrc = swd if shared else wd[ex]
        if not shared:
            for qt in range(NQT):
                wr_ld = P.dma("sync", lambda e, ex=ex, qt=qt: e.dma_start(
                    out=wrow[:], in_=wT_s[ex:ex + 1, qt * 512:(qt + 1) * 512]),
                    "dWrow",
                    waits=(wT_done + W(wrow_free)) if qt == 0 else W(wrow_free))
                P.op("tensor", lambda e: e.matmul(
                    pE[:], onesr_b[:], wrow[:], start=True, stop=True),
                    waits=[wr_ld] + W(wrep_free, ucp_free), inc="sPE")
                wr_mm = P.ms("sPE")
                wrow_free = wr_mm
                P.op("scalar", lambda e, qt=qt: e.copy(
                    wrep[:, qt * 512:(qt + 1) * 512], pE[:]),
                    waits=[wr_mm] + (W(prev_act_ms) if qt == 0 else []), inc="sSE")
                wrep_free = P.ms("sSE")
            wrep_ready = P.ms("sSE")
        else:
            wrep_ready = None
        for fc in range(NFC):
            sg = fc % 2
            su = 2
            g_ready = P.dma("gpsimd", lambda e, mg=mg, fc=fc, sg=sg: e.dma_start(
                out=wgu[sg][:].rearrange("p (j f) -> p j f", f=128),
                in_=wf_sp[mg, :, :, fc * 128:(fc + 1) * 128].transpose([1, 0, 2])),
                f"dGu{sg}", waits=W(gu_free[sg]))
            u_ready = P.dma("gpsimd", lambda e, mu=mu, fc=fc, su=su: e.dma_start(
                out=wgu[su][:].rearrange("p (j f) -> p j f", f=128),
                in_=wf_sp[mu, :, :, fc * 128:(fc + 1) * 128].transpose([1, 0, 2])),
                f"dGu{su}", waits=W(gu_free[su]))
            gu_ready = [g_ready, u_ready]
            for j in range(NJ):
                for qt in range(NQT):
                    P.op("tensor", lambda e, j=j, qt=qt, sg=sg: e.matmul(
                        pA[:, qt * 512:(qt + 1) * 512],
                        wgu[sg][:, j * 128:(j + 1) * 128],
                        h_sb[:, j * 2048 + qt * 512: j * 2048 + (qt + 1) * 512],
                        start=(j == 0), stop=(j == NJ - 1)),
                        waits=([g_ready] + W(pA_free4)) if (j == 0 and qt == 0) else (),
                        inc="sPE" if (j == NJ - 1 and qt == NQT - 1) else None)
            g_done = P.ms("sPE")
            P.op("scalar", lambda e, fc=fc: e.activation(
                act[:, fc * 2048:(fc + 1) * 2048], pA[:], AF.Silu),
                waits=[g_done] + W(act_free_fc[fc]), inc="sSE")
            pA_free4 = P.ms("sSE")
            up_w0 = [u_ready] + W(ucp_free)
            if fc == 0:
                up_w0 += wT_done if shared else [wrep_ready]
                up_w0 += W(pBC_free[0], pBC_free[1])
            for j in range(NJ):
                for qt in range(NQT):
                    P.op("tensor", lambda e, j=j, qt=qt, su=su: e.matmul(
                        UPB[qt][:],
                        wgu[su][:, j * 128:(j + 1) * 128],
                        h_sb[:, j * 2048 + qt * 512: j * 2048 + (qt + 1) * 512],
                        start=(j == 0), stop=(j == NJ - 1)),
                        waits=up_w0 if (j == 0 and qt == 0) else (),
                        inc="sPE" if (j == NJ - 1 and qt == NQT - 1) else None)
            u_done = P.ms("sPE")
            gu_free[sg] = u_done
            gu_free[su] = u_done
            for qt in range(NQT):
                P.op("scalar", lambda e, qt=qt: e.copy(
                    u_t[:, qt * 512:(qt + 1) * 512], UPB[qt][:]),
                    waits=([u_done] + W(ut_free)) if qt == 0 else (), inc="sSE")
            ucp_free = P.ms("sSE")
            P.op("vector", lambda e, fc=fc: e.tensor_tensor(
                act[:, fc * 2048:(fc + 1) * 2048], act[:, fc * 2048:(fc + 1) * 2048],
                u_t[:], ALU.mult), waits=[ucp_free], inc="sVE")
            if not shared:
                P.op("vector", lambda e, fc=fc: e.tensor_tensor(
                    act[:, fc * 2048:(fc + 1) * 2048], act[:, fc * 2048:(fc + 1) * 2048],
                    wrep[:], ALU.mult), waits=[wrep_ready], inc="sVE")
            act_ms = P.ms("sVE")
            ut_free = act_ms
            act_free_fc[fc] = None
        act_ready = P.ms("sVE")
        for od in range(NJ):
            ob = od % 2
            wdt_ld[ob] = P.dma("gpsimd", lambda e, od=od, ob=ob, wdsrc=wdsrc: e.dma_start(
                out=wdt[ob][:].rearrange("p (fc o) -> p fc o", o=128),
                in_=wdsrc[:, od * 128:(od + 1) * 128].rearrange("(fc p) o -> p fc o", p=128)),
                f"dWdt{ob}", waits=W(wdt_free[ob]))
            for qt in range(NQT):
                pp = pB if (od * NQT + qt) % 2 == 0 else pC
                ppi = (od * NQT + qt) % 2
                for fc in range(NFC):
                    P.op("tensor", lambda e, fc=fc, qt=qt, ob=ob, pp=pp: e.matmul(
                        pp[:], wdt[ob][:, fc * 128:(fc + 1) * 128],
                        act[:, fc * 2048 + qt * 512: fc * 2048 + (qt + 1) * 512],
                        start=(fc == 0), stop=(fc == NFC - 1)),
                        waits=([wdt_ld[ob], act_ready] + W(pBC_free[ppi]))
                              if fc == 0 else (),
                        inc="sPE" if fc == NFC - 1 else None)
                dmm = P.ms("sPE")
                if qt == NQT - 1:
                    wdt_free[ob] = dmm
                msl = moe[:, od * 2048 + qt * 512: od * 2048 + (qt + 1) * 512]
                if shared:
                    wmoe = [dmm]
                    if od == 0 and qt == 0:
                        wmoe = wmoe + wT_done
                    P.op("scalar", lambda e, msl=msl, pp=pp: e.copy(msl, pp[:]),
                         waits=wmoe, inc="sSE")
                    pBC_free[ppi] = P.ms("sSE")
                    moe_w.append(pBC_free[ppi])
                else:
                    P.op("scalar", lambda e, pp=pp: e.copy(d_f[0][:], pp[:]),
                         waits=[dmm] + W(df_free[0]), inc="sSE")
                    pBC_free[ppi] = P.ms("sSE")
                    dev = pBC_free[ppi]
                    P.op("vector", lambda e, msl=msl: e.tensor_tensor(
                        msl, msl, d_f[0][:], ALU.add), waits=[dev], inc="sVE")
                    df_free[0] = P.ms("sVE")
                    moe_w.append(df_free[0])
        alldown = P.ms("sPE")
        for fc in range(NFC):
            act_free_fc[fc] = alldown
        prev_act_ms = P.ms("sVE")

    P.barrier()

    al = mkalloc("p5", limit=MOE_OFF)
    x2pre = al("x2pre", (128, NJ * 512), BF16)
    x2f = al("x2f", (128, NJ * 512), BF16)
    xout = al("xout", (128, 2048), F32)
    t_f = [al(f"tf{i}", (128, 512), F32) for i in range(2)]
    rsrep_f = al("rsrep", (128, 512), F32)
    m2rep_f = al("m2rep", (128, 512), F32)
    r2rep_f = al("r2rep", (128, 512), F32)
    rows5 = al("rows5", (1, 4 * 512), F32)
    q_m = rows5[0:1, 0:512]; q_s = rows5[0:1, 512:1024]
    q_t = rows5[0:1, 1024:1536]; q_u = rows5[0:1, 1536:2048]
    sq5 = al("sq5", (128, 512), BF16)

    tf_free = [None, None]
    xout_free = None
    pA_free5 = None
    rep_free5 = None
    x2pre_prev = None
    sq5_free = None
    out_dmas = []

    for qt in range(NQT):
        P.op("tensor", lambda e, qt=qt: e.matmul(
            pD[:], onesr[:], rsinv_row[0:1, qt * 512:(qt + 1) * 512],
            start=True, stop=True), waits=W(rep_free5), inc="sPE")
        rsd = P.ms("sPE")
        P.op("scalar", lambda e: e.copy(rsrep_f[:], pD[:]),
             waits=[rsd] + W(x2pre_prev), inc="sSE")
        rsr = P.ms("sSE")
        for j in range(NJ):
            jb = j % 2
            P.op("vector", lambda e, j=j, jb=jb, qt=qt: e.tensor_tensor(
                t_f[jb][:], h_sb[:, j * 2048 + qt * 512: j * 2048 + (qt + 1) * 512],
                rsrep_f[:], ALU.mult), waits=[rsr] + W(tf_free[jb]), inc="sVE")
            P.op("vector", lambda e, j=j, jb=jb, qt=qt: e.tensor_tensor(
                x2pre[:, j * 512:(j + 1) * 512], t_f[jb][:],
                moe[:, j * 2048 + qt * 512: j * 2048 + (qt + 1) * 512], ALU.add),
                inc="sVE")
            x2_ms = P.ms("sVE")
            tf_free[jb] = x2_ms
            P.op("vector", lambda e, j=j: e.tensor_tensor(
                sq5[:], x2pre[:, j * 512:(j + 1) * 512],
                x2pre[:, j * 512:(j + 1) * 512], ALU.mult),
                waits=W(sq5_free), inc="sVE")
            sq_ms = P.ms("sVE")
            P.op("tensor", lambda e, j=j: e.matmul(
                pD[0:1, :], mean_c[:], x2pre[:, j * 512:(j + 1) * 512],
                start=(j == 0), stop=(j == NJ - 1)), waits=[x2_ms, rsd])
            P.op("tensor", lambda e, j=j: e.matmul(
                pE[0:1, :], mean_c[:], sq5[:],
                start=(j == 0), stop=(j == NJ - 1)), waits=[sq_ms],
                inc="sPE")
            sq5_free = P.ms("sPE")
        st_done = P.ms("sPE")
        P.op("scalar", lambda e: e.copy(q_m, pD[0:1, :]), waits=[st_done], inc="sSE")
        P.op("scalar", lambda e: e.copy(q_s, pE[0:1, :]), inc="sSE")
        rws = P.ms("sSE")
        P.op("vector", lambda e: e.tensor_tensor(q_t, q_m, q_m, ALU.mult),
             waits=[rws], inc="sVE")
        P.op("vector", lambda e: e.tensor_tensor(q_s, q_s, q_t, ALU.subtract), inc="sVE")
        P.op("vector", lambda e: e.tensor_scalar(q_u, q_s, 1e-5, None, ALU.add), inc="sVE")
        vr5 = P.ms("sVE")
        P.op("scalar", lambda e: e.activation(q_u, q_u, AF.Ln), waits=[vr5], inc="sSE")
        P.op("scalar", lambda e: e.activation(q_t, q_u, AF.Exp, scale=chn[0:1, 0:1]),
             inc="sSE")
        sr5 = P.ms("sSE")
        P.op("tensor", lambda e: e.matmul(pD[:], onesr[:], q_m, start=True, stop=True),
             waits=[sr5])
        P.op("tensor", lambda e: e.matmul(pE[:], onesr[:], q_t, start=True, stop=True),
             inc="sPE")
        rp5 = P.ms("sPE")
        P.op("scalar", lambda e: e.copy(m2rep_f[:], pD[:]),
             waits=[rp5] + W(x2pre_prev), inc="sSE")
        P.op("scalar", lambda e: e.copy(r2rep_f[:], pE[:]), inc="sSE")
        rpe5 = P.ms("sSE")
        rep_free5 = rpe5
        for j in range(NJ):
            jb = j % 2
            P.op("vector", lambda e, j=j, jb=jb: e.tensor_tensor(
                t_f[jb][:], x2pre[:, j * 512:(j + 1) * 512], m2rep_f[:], ALU.subtract),
                waits=[rpe5] + W(tf_free[jb]), inc="sVE")
            P.op("vector", lambda e, j=j, jb=jb: e.tensor_tensor(
                x2f[:, j * 512:(j + 1) * 512], t_f[jb][:], r2rep_f[:], ALU.mult),
                inc="sVE")
            tf_free[jb] = P.ms("sVE")
        x2f_done = P.ms("sVE")
        x2pre_prev = x2f_done
        for tc in range(4):
            for j in range(NJ):
                P.op("tensor", lambda e, j=j, tc=tc: e.transpose(
                    pA[:, j * 64:(j + 1) * 64].bitcast(BF16),
                    x2f[:, j * 512 + tc * 128: j * 512 + (tc + 1) * 128], sid_b[:]),
                    waits=([x2f_done] + W(pA_free5)) if j == 0 else (),
                    inc="sPE" if j == NJ - 1 else None)
            otp = P.ms("sPE")
            P.op("scalar", lambda e: e.copy(xout[:], pA[:, 0:1024].bitcast(BF16)),
                 waits=[otp] + W(xout_free), inc="sSE")
            pA_free5 = P.ms("sSE")
            xout_free = P.dma("sync", lambda e, qt=qt, tc=tc: e.dma_start(
                out=dout[(qt * 4 + tc) * 128:(qt * 4 + tc + 1) * 128, :], in_=xout[:]),
                "dOut", waits=[pA_free5])
            out_dmas.append(xout_free)

    P.barrier()

    with ExitStack() as stack:
        sems = {s: stack.enter_context(nc.semaphore(s)) for s in P.cnt}
        with nc.Block() as block:
            def runner(eng_name):
                def run(eng):
                    lastw = {}
                    for waits, fn, inc, amt in P.ops[eng_name]:
                        for sem, val in waits:
                            if lastw.get(sem, -1) < val:
                                eng.wait_ge(sems[sem], val)
                                lastw[sem] = val
                        if fn is None:
                            continue
                        instr = fn(eng)
                        if inc is not None:
                            instr.then_inc(sems[inc], amt)
                return run
            block.tensor(runner("tensor"))
            block.scalar(runner("scalar"))
            block.vector(runner("vector"))
            block.sync(runner("sync"))
            block.gpsimd(runner("gpsimd"))
    return nc


_NC_CACHE = {}


def kernel(L0_attn_hidden_states, L0_attn_Wq, L0_attn_Wk, L0_attn_Wv, L0_attn_Wo,
           L0_ffn_rms_weight, L0_ffn_router_w, L0_ffn_w_gate, L0_ffn_w_up,
           L0_ffn_w_down, L0_ffn_sw_gate, L0_ffn_sw_up, L0_ffn_sw_down):
    x = np.ascontiguousarray(np.asarray(L0_attn_hidden_states, dtype=np.float32))
    perm = np.concatenate([np.arange(0, HD, 2), np.arange(1, HD, 2)])

    def permute_cols(w, nh):
        w = np.asarray(w, dtype=np.float32)
        return np.ascontiguousarray(w.reshape(D, nh, HD)[:, :, perm].reshape(D, nh * HD))

    wq_p = permute_cols(L0_attn_Wq, NH)
    wk_p = permute_cols(L0_attn_Wk, NKV)
    wv = np.ascontiguousarray(np.asarray(L0_attn_Wv, dtype=np.float32))
    wo = np.ascontiguousarray(np.asarray(L0_attn_Wo, dtype=np.float32))
    rw = np.ascontiguousarray(np.asarray(L0_ffn_rms_weight, dtype=np.float32).reshape(D, 1))
    rtw = np.ascontiguousarray(np.asarray(L0_ffn_router_w, dtype=np.float32))
    wg = np.ascontiguousarray(np.asarray(L0_ffn_w_gate, dtype=np.float32))
    wu = np.ascontiguousarray(np.asarray(L0_ffn_w_up, dtype=np.float32))
    wd = np.ascontiguousarray(np.asarray(L0_ffn_w_down, dtype=np.float32))
    swg = np.ascontiguousarray(np.asarray(L0_ffn_sw_gate, dtype=np.float32))
    swu = np.ascontiguousarray(np.asarray(L0_ffn_sw_up, dtype=np.float32))
    swd = np.ascontiguousarray(np.asarray(L0_ffn_sw_down, dtype=np.float32))

    inv = 1.0 / THETA ** (np.arange(0, HD, 2, dtype=np.float64) / HD)
    tpos = np.arange(S, dtype=np.float64)
    fr = np.outer(tpos, inv)
    Cb, Sb = np.cos(fr), np.sin(fr)
    cc1 = np.concatenate([Cb, Cb], axis=1)
    ss1 = np.concatenate([-Sb, Sb], axis=1)
    rkc = np.ascontiguousarray(np.tile(cc1, (1, NKV)).astype(np.float32))
    rks = np.ascontiguousarray(np.tile(ss1, (1, NKV)).astype(np.float32))
    scale = HD ** -0.5
    rqc_full = np.tile(cc1 * scale, (1, NH)).astype(np.float32)
    rqs_full = np.tile(ss1 * scale, (1, NH)).astype(np.float32)
    idb = np.eye(128, dtype=ml_dtypes.bfloat16)
    idf = np.eye(128, dtype=np.float32)

    nc = build_nc()
    in_maps = []
    for c in range(8):
        b = c // 4
        qoff = (c % 4) * TME
        in_maps.append({
            "xfull": x[b], "xmine": np.ascontiguousarray(x[b, qoff:qoff + TME]),
            "wq": wq_p, "wk": wk_p, "wv": wv, "wo": wo, "rw": rw, "rtw": rtw,
            "wg": wg, "wu": wu, "wd": wd, "swg": swg, "swu": swu, "swd": swd,
            "rkc": rkc, "rks": rks,
            "rqc": np.ascontiguousarray(rqc_full[qoff:qoff + TME]),
            "rqs": np.ascontiguousarray(rqs_full[qoff:qoff + TME]),
            "idb": idb, "idf": idf,
        })
    res = run_bass_kernel_spmd(nc, in_maps, core_ids=list(range(8)))
    out = np.empty((2, S, D), dtype=np.float32)
    for c in range(8):
        b = c // 4
        qoff = (c % 4) * TME
        out[b, qoff:qoff + TME] = np.asarray(res.results[c]["out"], dtype=np.float32)
    return out
